# version 11 (final)
# baseline (speedup 1.0000x reference)
import contextlib
import numpy as np
import concourse.bass as bass
import concourse.mybir as mybir
from concourse.bass_utils import run_bass_kernel_spmd

F32 = mybir.dt.float32
BF16 = mybir.dt.bfloat16
I32 = mybir.dt.int32
AF = mybir.ActivationFunctionType
ALU = mybir.AluOpType
AX = mybir.AxisListType

S = 4096
E = 1024
NE = E // 128
H = 16
HD = 64
WIN = 128
FH = 3072
NF = FH // 128
DIN = 2048
ND = DIN // 128
DST = 16
DTR = 64
DCONV = 4
LN_EPS = 1e-5
QFS = 1.0 / np.sqrt(np.float32(HD))

NCORE = 8
Tc = 512
NCH = S // Tc
QT = Tc // 128

ENGINES = ("te", "dve", "act", "pool", "qs")
DMA_STREAMS = ("qs", "pool")
NDMASEM = 8


class Op:
    __slots__ = ("eng", "fn", "waits", "dma", "dma_cnt", "dma_sem_idx",
                 "dma_sem_use", "tag")

    def __init__(self, eng, fn, dma=False, tag=""):
        self.eng = eng
        self.fn = fn
        self.waits = {}
        self.dma = dma
        self.dma_cnt = 0
        self.tag = tag


class Ctx:

    def __init__(self, nc):
        self.nc = nc
        self.ops = {e: [] for e in ENGINES}
        self.res = {}
        self.dma_count = {e: 0 for e in DMA_STREAMS}

    def _r(self, name):
        return self.res.setdefault(name, {"writer": None, "readers": [], "late": False})

    def _add_wait(self, op, tok, force_same=False):
        kind, eng, cnt = tok
        if kind == "op" and eng == op.eng and not force_same:
            return
        key = (kind, eng)
        if op.waits.get(key, -1) < cnt:
            op.waits[key] = cnt

    def _pre(self, op, reads, writes, sreads=()):
        for nm in reads:
            r = self._r(nm)
            if r["writer"]:
                self._add_wait(op, r["writer"], force_same=r["late"])
        for nm in sreads:
            r = self._r(nm)
            if r["writer"]:
                self._add_wait(op, r["writer"], force_same=True)
        for nm in writes:
            r = self._r(nm)
            if r["writer"]:
                self._add_wait(op, r["writer"])
            for tok in r["readers"]:
                self._add_wait(op, tok)

    def _post(self, tok, reads, writes):
        for nm in reads:
            self._r(nm)["readers"].append(tok)
        for nm in writes:
            r = self._r(nm)
            r["writer"] = tok
            r["readers"] = []

    def op(self, eng, fn, reads=(), writes=(), sreads=(), late=(), tag=""):
        o = Op(eng, fn, tag=tag)
        self._pre(o, reads, writes, sreads=sreads)
        self.ops[eng].append(o)
        tok = ("op", eng, len(self.ops[eng]))
        self._post(tok, tuple(reads) + tuple(sreads), writes)
        for nm in late:
            self._r(nm)["late"] = True
        return o

    def barrier(self):
        snap_ops = {e: len(self.ops[e]) for e in ENGINES}
        snap_dma = {e: self.dma_count[e] for e in DMA_STREAMS}
        for e in ENGINES:
            o = Op(e, lambda h: h.nop(), tag="barrier")
            for e2 in ENGINES:
                if e2 != e and snap_ops[e2]:
                    o.waits[("op", e2)] = snap_ops[e2]
            for e2 in DMA_STREAMS:
                n = snap_dma[e2]
                for sidx in range(min(n, NDMASEM)):
                    uses = (n - 1 - sidx) // NDMASEM + 1
                    o.waits[("dma", (e2, sidx))] = 16 * uses
            self.ops[e].append(o)

    def dma(self, eng, out, in_, reads=(), writes=(), tag=""):
        assert eng in DMA_STREAMS
        o = Op(eng, None, dma=True, tag=tag)
        self._pre(o, reads, writes)
        self.dma_count[eng] += 1
        idx = (self.dma_count[eng] - 1) % NDMASEM
        use = (self.dma_count[eng] - 1) // NDMASEM + 1
        o.dma_cnt = self.dma_count[eng]
        o.dma_sem_idx = idx
        o.dma_sem_use = use

        def fn(h, _o=out, _i=in_):
            return h.dma_start(out=_o, in_=_i)
        o.fn = fn
        self.ops[eng].append(o)
        tok = ("dma", (eng, idx), 16 * use)
        self._post(tok, reads, writes)
        return o

    def finalize(self, block, sems, dma_sems):
        import bisect as _bisect
        nondma = {e: [i + 1 for i, o in enumerate(self.ops[e]) if not o.dma]
                  for e in ENGINES}

        def snap_idx(eng, cnt):
            lst = nondma[eng]
            k = _bisect.bisect_right(lst, cnt)
            return lst[k - 1] if k else 0

        for e in ENGINES:
            for o in self.ops[e]:
                newwaits = {}
                for (kind, eng), cnt in o.waits.items():
                    if kind == "op":
                        cnt = snap_idx(eng, cnt)
                        if cnt == 0:
                            continue
                    newwaits[(kind, eng)] = max(newwaits.get((kind, eng), -1), cnt)
                o.waits = newwaits
        need_inc = {e: set() for e in ENGINES}
        for e in ENGINES:
            for o in self.ops[e]:
                for (kind, eng), cnt in o.waits.items():
                    if kind == "op":
                        need_inc[eng].add(cnt)
        for e in ENGINES:
            if nondma[e]:
                need_inc[e].add(nondma[e][-1])

        handles = {"te": self.nc.tensor, "dve": self.nc.vector,
                   "act": self.nc.scalar, "pool": self.nc.gpsimd,
                   "qs": self.nc.sync}

        import bisect
        inc_sorted = {e: sorted(need_inc[e]) for e in ENGINES}

        def sem_val(eng, idx):
            return bisect.bisect_right(inc_sorted[eng], idx)

        def emit(e):
            h = handles[e]
            waited = {}
            for i, o in enumerate(self.ops[e]):
                if o.dma and o.dma_sem_use > 1:
                    key = ("dma", (e, o.dma_sem_idx))
                    prev = 16 * (o.dma_sem_use - 1)
                    if waited.get(key, -1) < prev:
                        h.wait_ge(dma_sems[e][o.dma_sem_idx], prev)
                        waited[key] = prev
                for key in sorted(o.waits, key=str):
                    cnt = o.waits[key]
                    if waited.get(key, -1) >= cnt:
                        continue
                    kind, eng = key
                    if kind == "op":
                        h.wait_ge(sems[eng], sem_val(eng, cnt))
                    else:
                        sename, sidx = eng
                        h.wait_ge(dma_sems[sename][sidx], cnt)
                    waited[key] = cnt
                inst = o.fn(h)
                if o.dma:
                    inst.then_inc(dma_sems[e][o.dma_sem_idx], 16)
                elif (i + 1) in need_inc[e]:
                    inst.then_inc(sems[e], 1)
            if e in DMA_STREAMS and self.dma_count[e]:
                n = self.dma_count[e]
                for sidx in range(min(n, NDMASEM)):
                    uses = (n - 1 - sidx) // NDMASEM + 1
                    h.wait_ge(dma_sems[e][sidx], 16 * uses)

        @block.tensor
        def _(te):
            emit("te")

        @block.vector
        def _(v):
            emit("dve")

        @block.scalar
        def _(s):
            emit("act")

        @block.gpsimd
        def _(g):
            emit("pool")

        @block.sync
        def _(sy):
            emit("qs")


class Prog:

    def __init__(self, name="prog"):
        self.nc = bass.Bass("TRN2", target_bir_lowering=False)
        self.stack = contextlib.ExitStack()
        self.ctx = Ctx(self.nc)
        self.psum = []
        self._psum_bf16 = None

    def __enter__(self):
        self.stack.__enter__()
        self.psum = [self.stack.enter_context(
            self.nc.psum_tensor(f"pb{i}", [128, 512], F32)) for i in range(7)]
        self._psum_bf16 = self.stack.enter_context(
            self.nc.psum_tensor("pb7", [128, 1024], BF16))
        self.sems = {e: self.stack.enter_context(self.nc.semaphore(f"sem_{e}"))
                     for e in ENGINES}
        self.dma_sems = {e: [self.stack.enter_context(self.nc.semaphore(f"dsem_{e}{i}"))
                             for i in range(NDMASEM)]
                         for e in DMA_STREAMS}
        return self

    def sb(self, name, shape, dt):
        st = getattr(self, "scopes", None)
        tgt = st[-1] if st else self.stack
        self._uniq = getattr(self, "_uniq", 0) + 1
        return tgt.enter_context(self.nc.sbuf_tensor(f"{name}_u{self._uniq}", shape, dt))

    def push_scope(self):
        if not hasattr(self, "scopes"):
            self.scopes = []
        self.scopes.append(contextlib.ExitStack())
        self.scopes[-1].__enter__()

    def pop_scope(self):
        self.scopes.pop().__exit__(None, None, None)

    def din(self, name, shape, dt):
        return self.nc.dram_tensor(name, shape, dt, kind="ExternalInput").ap()

    def dout(self, name, shape, dt):
        return self.nc.dram_tensor(name, shape, dt, kind="ExternalOutput").ap()

    def close(self):
        block = self.stack.enter_context(self.nc.Block())
        self.ctx.finalize(block, self.sems, self.dma_sems)
        self.stack.__exit__(None, None, None)
        return self.nc



def constf(p, v):
    if v not in p.consts:
        t = p.sb(f"cst{len(p.consts)}", [128, 1], F32)
        p.ctx.op("pool", lambda g, t=t, v=v: g.memset(t[:], float(v)), writes=["consts"])
        p.consts[v] = t
    return p.consts[v]


def setup_common(p):
    if hasattr(p, "ones"):
        return
    ctx = p.ctx
    p.ones = p.sb("ones", [128, 128], BF16)
    ctx.op("pool", lambda g: g.memset(p.ones[:], 1.0), writes=["ones"])
    p.ones_f = p.sb("ones_f", [1, 128], F32)
    ctx.op("pool", lambda g: g.memset(p.ones_f[:], 1.0), writes=["ones"])
    p.consts = {}
    p.ident_f = p.sb("ident_f", [128, 128], F32)
    p.ident = p.sb("ident", [128, 128], BF16)
    ctx.op("pool", lambda g: g.memset(p.ident_f[:], 1.0), writes=["identf"])
    ctx.op("pool", lambda g: g.affine_select(p.ident_f[:], p.ident_f[:], [[1, 128]],
                                             ALU.is_equal, 0.0, base=0,
                                             channel_multiplier=-1),
           reads=["identf"], writes=["identf"])
    ctx.op("dve", lambda v: v.tensor_copy(p.ident[:], p.ident_f[:]),
           reads=["identf"], writes=["ident"])


def emit_ln(p, pre, x_tiles, x_res, out_tiles, out_res, scr, gamma=None, beta=None,
            n=NE, width=Tc, stats_from=None, xbf_is_out=False):
    ctx = p.ctx
    nE = n
    if stats_from is None:
        xbf = out_tiles if xbf_is_out else scr["xbf"]
        alias = ([f"{out_res}{j}" for j in range(nE)] if xbf_is_out else [])
        for j in range(nE):
            ctx.op("dve", lambda v, j=j: v.tensor_copy(xbf[j][:, :width], x_tiles[j][:, :width]),
                   reads=[f"{x_res}{j}"],
                   writes=[f"{pre}xbf{j}"] + ([alias[j]] if alias else []))
        sf = xbf
        sf_res = [f"{pre}xbf{j}" for j in range(nE)]
    else:
        sf = stats_from
        sf_res = [f"{x_res}{j}" for j in range(nE)]
    psA, psB, psC = scr["psA"], scr["psB"], scr["psC"]
    xsq, mrow, vrow, rrow = scr["xsq"], scr["mrow"], scr["vrow"], scr["rrow"]
    mb, rb = scr["mb"], scr["rb"]
    psA_r, psB_r, psC_r = scr["psA_r"], scr["psB_r"], scr["psC_r"]

    for j in range(nE):
        ctx.op("te", lambda t, j=j: t.matmul(psA[0:1, :width], p.ones[:, 0:1],
                                             sf[j][:, :width],
                                             start=(j == 0), stop=(j == nE - 1)),
               reads=[sf_res[j], "ones"], writes=[psA_r])
    for j in range(nE):
        ctx.op("act", lambda s, j=j: s.activation(xsq[:, :width], sf[j][:, :width], AF.Square),
               reads=[sf_res[j]], writes=[f"{pre}xsq"])
        ctx.op("te", lambda t, j=j: t.matmul(psB[0:1, :width], p.ones[:, 0:1], xsq[:, :width],
                                             start=(j == 0), stop=(j == nE - 1)),
               reads=[f"{pre}xsq", "ones"], writes=[psB_r])
    ctx.op("act", lambda s: s.activation(mrow[:, :width], psA[0:1, :width], AF.Copy,
                                         scale=1.0 / (128 * nE)),
           reads=[psA_r], writes=[f"{pre}mrow"])
    ctx.op("dve", lambda v: v.scalar_tensor_tensor(vrow[:, :width], mrow[:, :width], -1.0,
                                                   mrow[:, :width], ALU.mult, ALU.mult),
           reads=[f"{pre}mrow"], writes=[f"{pre}vrow"])
    ctx.op("dve", lambda v: v.scalar_tensor_tensor(vrow[:, :width], psB[0:1, :width],
                                                   1.0 / (128 * nE), vrow[:, :width],
                                                   ALU.mult, ALU.add),
           reads=[psB_r, f"{pre}vrow"], writes=[f"{pre}vrow"])
    ctx.op("act", lambda s: s.activation(rrow[:, :width], vrow[:, :width], AF.Ln,
                                         bias=np.float32(LN_EPS)),
           reads=[f"{pre}vrow"], writes=[f"{pre}rrow"])
    ctx.op("act", lambda s: s.activation(rrow[:, :width], rrow[:, :width], AF.Exp,
                                         scale=-0.5),
           reads=[f"{pre}rrow"], writes=[f"{pre}rrow"])
    ctx.op("te", lambda t: t.matmul(psC[:, :width], p.ones_f[0:1, :], mrow[:, :width],
                                    start=True, stop=True),
           reads=[f"{pre}mrow", "ones"], writes=[psC_r])
    ctx.op("act", lambda s: s.copy(mb[:, :width], psC[:, :width]),
           reads=[psC_r], writes=[f"{pre}mb"])
    ctx.op("te", lambda t: t.matmul(psC[:, :width], p.ones_f[0:1, :], rrow[:, :width],
                                    start=True, stop=True),
           reads=[f"{pre}rrow", "ones"], writes=[psC_r])
    ctx.op("act", lambda s: s.copy(rb[:, :width], psC[:, :width]),
           reads=[psC_r], writes=[f"{pre}rb"])
    t1 = scr["t1"]
    for j in range(nE):
        ctx.op("dve", lambda v, j=j: v.tensor_sub(t1[:, :width], x_tiles[j][:, :width],
                                                  mb[:, :width]),
               reads=[f"{x_res}{j}", f"{pre}mb"], writes=[f"{pre}t1"])
        if gamma is None:
            ctx.op("dve", lambda v, j=j: v.tensor_mul(out_tiles[j][:, :width], t1[:, :width],
                                                      rb[:, :width]),
                   reads=[f"{pre}t1", f"{pre}rb"],
                   writes=[f"{out_res}{j}"] + ([f"{pre}xbf{j}"] if xbf_is_out else []))
        else:
            t2 = scr["t2"]
            ctx.op("dve", lambda v, j=j: v.tensor_mul(t2[:, :width], t1[:, :width],
                                                      rb[:, :width]),
                   reads=[f"{pre}t1", f"{pre}rb"], writes=[f"{pre}t2"])
            ctx.op("dve", lambda v, j=j: v.tensor_scalar(
                out_tiles[j][:, :width], t2[:, :width], gamma[:, j:j + 1], beta[:, j:j + 1],
                ALU.mult, ALU.add),
                reads=[f"{pre}t2", "gmbt"],
                writes=[f"{out_res}{j}"] + ([f"{pre}xbf{j}"] if xbf_is_out else []))


def emit_ln_residual(p, pre, f_tiles, f_res, x_tiles, x_res, scr, width=Tc):
    ctx = p.ctx
    emit_ln_stats_only(p, pre, f_tiles, f_res, scr, width)
    t1 = scr["t1"]
    mb, rb = scr["mb"], scr["rb"]
    for j in range(NE):
        ctx.op("dve", lambda v, j=j: v.tensor_sub(t1[:, :width], f_tiles[j][:, :width],
                                                  mb[:, :width]),
               reads=[f"{f_res}{j}", f"{pre}mb"], writes=[f"{pre}t1"])
        ctx.op("dve", lambda v, j=j: v.tensor_mul(t1[:, :width], t1[:, :width],
                                                  rb[:, :width]),
               reads=[f"{pre}t1", f"{pre}rb"], writes=[f"{pre}t1"])
        ctx.op("dve", lambda v, j=j: v.tensor_add(x_tiles[j][:, :width], x_tiles[j][:, :width],
                                                  t1[:, :width]),
               reads=[f"{pre}t1", f"{x_res}{j}"], writes=[f"{x_res}{j}"])


def emit_ln_stats_only(p, pre, sf, sf_res_pref, scr, width=Tc, n=NE):
    ctx = p.ctx
    psA, psB, psC = scr["psA"], scr["psB"], scr["psC"]
    psA_r, psB_r, psC_r = scr["psA_r"], scr["psB_r"], scr["psC_r"]
    xsq, mrow, vrow, rrow = scr["xsq"], scr["mrow"], scr["vrow"], scr["rrow"]
    mb, rb = scr["mb"], scr["rb"]
    for j in range(n):
        ctx.op("te", lambda t, j=j: t.matmul(psA[0:1, :width], p.ones[:, 0:1],
                                             sf[j][:, :width],
                                             start=(j == 0), stop=(j == n - 1)),
               reads=[f"{sf_res_pref}{j}"], writes=[psA_r])
    for j in range(n):
        ctx.op("act", lambda s, j=j: s.activation(xsq[:, :width], sf[j][:, :width], AF.Square),
               reads=[f"{sf_res_pref}{j}"], writes=[f"{pre}xsq"])
        ctx.op("te", lambda t, j=j: t.matmul(psB[0:1, :width], p.ones[:, 0:1], xsq[:, :width],
                                             start=(j == 0), stop=(j == n - 1)),
               reads=[f"{pre}xsq"], writes=[psB_r])
    ctx.op("act", lambda s: s.activation(mrow[:, :width], psA[0:1, :width], AF.Copy,
                                         scale=1.0 / (128 * n)),
           reads=[psA_r], writes=[f"{pre}mrow"])
    ctx.op("dve", lambda v: v.scalar_tensor_tensor(vrow[:, :width], mrow[:, :width], -1.0,
                                                   mrow[:, :width], ALU.mult, ALU.mult),
           reads=[f"{pre}mrow"], writes=[f"{pre}vrow"])
    ctx.op("dve", lambda v: v.scalar_tensor_tensor(vrow[:, :width], psB[0:1, :width],
                                                   1.0 / (128 * n), vrow[:, :width],
                                                   ALU.mult, ALU.add),
           reads=[psB_r, f"{pre}vrow"], writes=[f"{pre}vrow"])
    ctx.op("act", lambda s: s.activation(rrow[:, :width], vrow[:, :width], AF.Ln,
                                         bias=np.float32(LN_EPS)),
           reads=[f"{pre}vrow"], writes=[f"{pre}rrow"])
    ctx.op("act", lambda s: s.activation(rrow[:, :width], rrow[:, :width], AF.Exp,
                                         scale=-0.5),
           reads=[f"{pre}rrow"], writes=[f"{pre}rrow"])
    ctx.op("te", lambda t: t.matmul(psC[:, :width], p.ones_f[0:1, :], mrow[:, :width],
                                    start=True, stop=True),
           reads=[f"{pre}mrow", "ones"], writes=[psC_r])
    ctx.op("act", lambda s: s.copy(mb[:, :width], psC[:, :width]),
           reads=[psC_r], writes=[f"{pre}mb"])
    ctx.op("te", lambda t: t.matmul(psC[:, :width], p.ones_f[0:1, :], rrow[:, :width],
                                    start=True, stop=True),
           reads=[f"{pre}rrow", "ones"], writes=[psC_r])
    ctx.op("act", lambda s: s.copy(rb[:, :width], psC[:, :width]),
           reads=[psC_r], writes=[f"{pre}rb"])


def make_ln_scratch(p, tag="", width=None, no_xbf=False):
    w = width or Tc
    scr = {
        "xbf": (None if no_xbf else
                [p.sb(f"lnxbf{tag}{j}", [128, w], BF16) for j in range(NE)]),
        "xsq": p.sb(f"lnxsq{tag}", [128, w], BF16),
        "mrow": p.sb(f"lnm{tag}", [1, w], F32),
        "vrow": p.sb(f"lnv{tag}", [1, w], F32),
        "rrow": p.sb(f"lnr{tag}", [1, w], F32),
        "mb": p.sb(f"lnmb{tag}", [128, w], F32),
        "rb": p.sb(f"lnrb{tag}", [128, w], F32),
        "t1": p.sb(f"lnt1{tag}", [128, w], F32),
        "t2": p.sb(f"lnt2{tag}", [128, w], F32),
    }
    return scr



def build_stage_l0attn(debug=99.0, p=None, io=None):
    standalone = p is None
    if standalone:
        p = Prog("l0attn").__enter__()
    nc, ctx = p.nc, p.ctx
    xT = io["xT"] if io else p.din("xT", [E, S], F32)
    w_qkT = p.din("w_qkT", [E, 2048], F32)
    w_vT = p.din("w_vT", [E, 1024], F32)
    b_inP = p.din("b_inP", [128, 24], F32)
    w_oT = p.din("w_oT", [E, E], F32)
    b_oP = p.din("b_oP", [128, 8], F32)
    yT = io["yT"] if io else p.dout("yT", [E, S], F32)

    setup_common(p)
    PS = p.psum
    psT = p._psum_bf16

    wqk = p.sb("wqk", [128, NE * 2048], BF16)
    wv = p.sb("wv", [128, NE * 1024], BF16)
    wo = p.sb("wo", [128, NE * 1024], BF16)
    bin_sb = p.sb("bin_sb", [128, 24], F32)
    bo_sb = p.sb("bo_sb", [128, 8], F32)
    for j in range(NE):
        ctx.dma("pool", wqk[:, j * 2048:(j + 1) * 2048], w_qkT[j * 128:(j + 1) * 128, :],
                writes=[f"wqk{j}"])
        ctx.dma("pool", wv[:, j * 1024:(j + 1) * 1024], w_vT[j * 128:(j + 1) * 128, :],
                writes=[f"wv{j}"])
        ctx.dma("pool", wo[:, j * 1024:(j + 1) * 1024], w_oT[j * 128:(j + 1) * 128, :],
                writes=[f"wo{j}"])
    ctx.dma("qs", bin_sb[:], b_inP, writes=["bin"])
    ctx.dma("qs", bo_sb[:], b_oP, writes=["bo"])

    mask2 = p.sb("mask2", [128, 512], F32)
    ctx.op("pool", lambda g: g.memset(mask2[:], 0.0), writes=["mask2"])
    for hh in range(2):
        ctx.op("pool", lambda g, hh=hh: g.affine_select(
            mask2[:, hh * 256:(hh + 1) * 256], mask2[:, hh * 256:(hh + 1) * 256],
            [[1, 256]], ALU.is_ge, -1e9, base=0, channel_multiplier=-1),
            reads=["mask2"], writes=["mask2"])
        ctx.op("pool", lambda g, hh=hh: g.affine_select(
            mask2[:, hh * 256:(hh + 1) * 256], mask2[:, hh * 256:(hh + 1) * 256],
            [[-1, 256]], ALU.is_ge, -1e9, base=WIN, channel_multiplier=1),
            reads=["mask2"], writes=["mask2"])
    mask2f = p.sb("mask2f", [128, 512], F32)
    ctx.op("pool", lambda g: g.tensor_copy(mask2f[:], mask2[:]),
           reads=["mask2"], writes=["mask2f"])
    for hh in range(2):
        ctx.op("pool", lambda g, hh=hh: g.affine_select(
            mask2f[:, hh * 256:(hh + 1) * 256], mask2f[:, hh * 256:(hh + 1) * 256],
            [[1, 256]], ALU.is_ge, -1e9, base=-128, channel_multiplier=0),
            reads=["mask2f"], writes=["mask2f"])

    xt = [p.sb(f"xt{j}", [128, Tc], F32) for j in range(NE)]
    lnx = [p.sb(f"lnxx{j}", [128, Tc], BF16) for j in range(NE)]
    scr = make_ln_scratch(p)
    scr["psA"], scr["psB"], scr["psC"] = PS[4], PS[5], PS[6]
    scr["psA_r"], scr["psB_r"], scr["psC_r"] = "ps4", "ps5", "ps6"

    qh = p.sb("qh", [64, H * Tc], BF16)
    KW = WIN + Tc
    kh = p.sb("kh", [64, H * KW], BF16)
    NV = (WIN + Tc) // 128
    v_sb = p.sb("v_sb", [128, NV * 1024], BF16)
    NPAR = 4
    sm = [p.sb(f"sm{i}", [128, 512], F32) for i in range(NPAR)]
    nmax = [p.sb(f"nmax{i}", [128, 1], F32) for i in range(NPAR)]
    den = [p.sb(f"den{i}", [128, 2], F32) for i in range(NPAR)]
    rden = [p.sb(f"rden{i}", [128, 2], F32) for i in range(NPAR)]
    p_scr = [p.sb(f"p_scr{i}", [128, 512], BF16) for i in range(NPAR)]
    p_sb = [p.sb(f"p_sb{i}", [128, 512], BF16) for i in range(NPAR)]
    pt_sb = [p.sb(f"pt_sb{i}", [128, 512], BF16) for i in range(NPAR)]
    fbf = [p.sb(f"fbf{j}", [128, Tc], BF16) for j in range(NE)]
    obf = [p.sb(f"obf{j}", [128, Tc], BF16) for j in range(NE)]

    ctx.op("pool", lambda g: g.memset(kh[:], 0.0), writes=["kh"])
    ctx.op("pool", lambda g: g.memset(v_sb[:, 0:1024], 0.0), writes=["vch"])

    for c in range(NCH):
        t0 = c * Tc
        for j in range(NE):
            ctx.dma("qs", xt[j][:], xT[j * 128:(j + 1) * 128, t0:t0 + Tc],
                    writes=[f"x{j}"])
        emit_ln(p, f"L{c}_", xt, "x", lnx, "lnx", scr)

        if c > 0:
            ctx.op("dve", lambda v: v.tensor_copy(
                kh[:].rearrange("p (h w) -> p h w", h=H)[:, :, 0:WIN],
                kh[:].rearrange("p (h w) -> p h w", h=H)[:, :, Tc:Tc + WIN]),
                reads=["kh"], writes=["kh"])
            ctx.op("dve", lambda v: v.tensor_copy(v_sb[:, 0:1024],
                                                  v_sb[:, (NV - 1) * 1024:NV * 1024]),
                   reads=["vch"], writes=["vch"])

        for fo in (range(NE) if debug >= 1 else []):
            pb = PS[fo % 2]
            pr = f"ps{fo % 2}"
            for j in range(NE):
                ctx.op("te", lambda t, fo=fo, j=j, pb=pb: t.matmul(
                    pb[:], wqk[:, j * 2048 + fo * 128: j * 2048 + (fo + 1) * 128],
                    lnx[j][:, :], start=(j == 0), stop=(j == NE - 1)),
                    reads=[f"lnx{j}", f"wqk{j}"], writes=[pr])
            for hh in range(2):
                ctx.op("act", lambda s, fo=fo, pb=pb, hh=hh: s.activation(
                    qh[:, (2 * fo + hh) * Tc:(2 * fo + hh + 1) * Tc],
                    pb[hh * 64:(hh + 1) * 64, :], AF.Identity,
                    bias=bin_sb[hh * 64:(hh + 1) * 64, fo:fo + 1]),
                    reads=[pr, "bin"], writes=["qhh"])
        for fo in (range(NE) if debug >= 1 else []):
            pb = PS[fo % 2]
            pr = f"ps{fo % 2}"
            for j in range(NE):
                ctx.op("te", lambda t, fo=fo, j=j, pb=pb: t.matmul(
                    pb[:], wqk[:, j * 2048 + 1024 + fo * 128: j * 2048 + 1024 + (fo + 1) * 128],
                    lnx[j][:, :], start=(j == 0), stop=(j == NE - 1)),
                    reads=[f"lnx{j}", f"wqk{j}"], writes=[pr])
            for hh in range(2):
                ctx.op("act", lambda s, fo=fo, pb=pb, hh=hh: s.activation(
                    kh[:, (2 * fo + hh) * KW + WIN:(2 * fo + hh) * KW + WIN + Tc],
                    pb[hh * 64:(hh + 1) * 64, :], AF.Identity,
                    bias=bin_sb[hh * 64:(hh + 1) * 64, 8 + fo:9 + fo]),
                    reads=[pr, "bin"], writes=["kh"])
        if debug < 1:
            for j in range(NE):
                ctx.op("dve", lambda v, j=j: v.tensor_copy(q_sb[j][:], lnx[j][:]),
                       reads=[f"lnx{j}"], writes=[f"qch{j}"])
        for vt in (range(QT) if debug >= 2 else []):
            for half in range(2):
                pb = PS[2 + half]
                pr = f"ps{2 + half}"
                for j in range(NE):
                    ctx.op("te", lambda t, vt=vt, half=half, j=j, pb=pb: t.matmul(
                        pb[:], lnx[j][:, vt * 128:(vt + 1) * 128],
                        wv[:, j * 1024 + half * 512: j * 1024 + (half + 1) * 512],
                        start=(j == 0), stop=(j == NE - 1)),
                        reads=[f"lnx{j}", f"wv{j}"], writes=[pr])
                ctx.op("act", lambda s, vt=vt, half=half, pb=pb: s.copy(
                    v_sb[:, (vt + 1) * 1024 + half * 512:(vt + 1) * 1024 + (half + 1) * 512],
                    pb[:]),
                    reads=[pr], writes=["vch"])

        if debug < 2.25:
            for j in range(NE):
                ctx.op("dve", lambda v, j=j: v.tensor_copy(fbf[j][:], lnx[j][:]),
                       reads=[f"lnx{j}"], writes=[f"fbf{j}"])
        for qt in (range(QT) if debug >= 2.25 else []):
            msk = mask2f if (c == 0 and qt == 0) else mask2
            msk_r = "mask2f" if (c == 0 and qt == 0) else "mask2"
            for ht in range(NE):
                sps = PS[2 + (ht % 2)]
                sps_r = f"ps{2 + (ht % 2)}"
                for hh in range(2):
                    head = ht * 2 + hh
                    ctx.op("te", lambda t, qt=qt, head=head, hh=hh, sps=sps: t.matmul(
                        sps[:, hh * 256:(hh + 1) * 256],
                        qh[:, head * Tc + qt * 128: head * Tc + (qt + 1) * 128],
                        kh[:, head * KW + qt * 128: head * KW + qt * 128 + 256],
                        start=True, stop=True),
                        reads=["qhh", "kh"], writes=[sps_r])
                bp = (qt * NE + ht) % NPAR
                smb, nmb, denb, rdenb = sm[bp], nmax[bp], den[bp], rden[bp]
                pscrb, psbb, ptb = p_scr[bp], p_sb[bp], pt_sb[bp]
                ctx.op("dve", lambda v, sps=sps, nmb=nmb: v.reduce_max(
                    nmb[:], sps[:], axis=AX.X, negate=True),
                    reads=[sps_r], writes=[f"nmax{bp}"], late=[f"nmax{bp}"])
                ctx.op("dve", lambda v, sps=sps, msk=msk, smb=smb, nmb=nmb:
                       v.scalar_tensor_tensor(
                           smb[:], sps[:], nmb[:, 0:1], msk[:], ALU.add, ALU.add),
                       reads=[sps_r, msk_r], sreads=[f"nmax{bp}"], writes=[f"sm{bp}"])
                for hh in range(2):
                    ctx.op("act", lambda s, hh=hh, smb=smb, pscrb=pscrb, denb=denb:
                           s.activation(
                               pscrb[:, hh * 256:(hh + 1) * 256],
                               smb[:, hh * 256:(hh + 1) * 256], AF.Exp, scale=QFS,
                               accum_out=denb[:, hh:hh + 1]),
                           reads=[f"sm{bp}"], writes=[f"p_scr{bp}", f"den{bp}"],
                           late=[f"den{bp}"])
                ctx.op("dve", lambda v, denb=denb, rdenb=rdenb: v.reciprocal(
                    rdenb[:], denb[:]),
                    reads=[f"den{bp}"], writes=[f"rden{bp}"], late=[f"rden{bp}"])
                for hh in range(2):
                    ctx.op("dve", lambda v, hh=hh, pscrb=pscrb, psbb=psbb, rdenb=rdenb:
                           v.tensor_scalar_mul(
                               psbb[:, hh * 256:(hh + 1) * 256],
                               pscrb[:, hh * 256:(hh + 1) * 256], rdenb[:, hh:hh + 1]),
                           reads=[f"p_scr{bp}"], sreads=[f"rden{bp}"],
                           writes=[f"p_sb{bp}"])
                for hh in (range(2) if debug >= 2.5 else []):
                    for kt in range(2):
                        ctx.op("te", lambda t, hh=hh, kt=kt, psbb=psbb: t.transpose(
                            psT[:, (hh * 2 + kt) * 128:(hh * 2 + kt + 1) * 128],
                            psbb[:, hh * 256 + kt * 128: hh * 256 + (kt + 1) * 128],
                            p.ident[:]),
                            reads=[f"p_sb{bp}", "ident"], writes=["ps7"])
                if debug >= 2.5:
                    ctx.op("act", lambda s, ptb=ptb: s.copy(ptb[:], psT[:, 0:512]),
                           reads=["ps7"], writes=[f"pt{bp}"])
                ob = PS[4 + (ht % 2)]
                ob_r = f"ps{4 + (ht % 2)}"
                for hh in (range(2) if debug >= 2.75 else []):
                    head = ht * 2 + hh
                    for kt in range(2):
                        vtile = qt + kt
                        ctx.op("te", lambda t, hh=hh, kt=kt, vtile=vtile, head=head, ob=ob, qt=qt, ptb=ptb: t.matmul(
                            ob[hh * 64:(hh + 1) * 64, qt * 128:(qt + 1) * 128],
                            v_sb[:, vtile * 1024 + head * 64: vtile * 1024 + (head + 1) * 64],
                            ptb[:, (hh * 2 + kt) * 128:(hh * 2 + kt + 1) * 128],
                            start=(kt == 0), stop=(kt == 1)),
                            reads=[f"pt{bp}", "vch"], writes=[ob_r])
                if debug >= 2.75:
                    ctx.op("act", lambda s, ht=ht, qt=qt, ob=ob: s.activation(
                        fbf[ht][:, qt * 128:(qt + 1) * 128], ob[:, qt * 128:(qt + 1) * 128],
                        AF.Identity, bias=bin_sb[:, 16 + ht:17 + ht]),
                        reads=[ob_r, "bin"], writes=[f"fbf{ht}"])
                else:
                    ctx.op("dve", lambda v, ht=ht, qt=qt: v.tensor_copy(
                        fbf[ht][:, qt * 128:(qt + 1) * 128],
                        p_sb[:, 0:128]),
                        reads=["p_sb"], writes=[f"fbf{ht}"])

        for fo in (range(NE) if debug >= 3 else []):
            pb = PS[fo % 2]
            pr = f"ps{fo % 2}"
            for j in range(NE):
                ctx.op("te", lambda t, fo=fo, j=j, pb=pb: t.matmul(
                    pb[:], wo[:, j * 1024 + fo * 128: j * 1024 + (fo + 1) * 128],
                    fbf[j][:, :], start=(j == 0), stop=(j == NE - 1)),
                    reads=[f"fbf{j}", f"wo{j}"], writes=[pr])
            ctx.op("act", lambda s, fo=fo, pb=pb: s.activation(
                obf[fo][:], pb[:], AF.Identity, bias=bo_sb[:, fo:fo + 1]),
                reads=[pr, "bo"], writes=[f"obf{fo}"])

        if debug < 3:
            for j in range(NE):
                ctx.op("dve", lambda v, j=j: v.tensor_copy(obf[j][:], fbf[j][:]),
                       reads=[f"fbf{j}"], writes=[f"obf{j}"])
        emit_ln_residual(p, f"R{c}_", obf, "obf", xt, "x", scr)
        for j in range(NE):
            ctx.dma("qs", yT[j * 128:(j + 1) * 128, t0:t0 + Tc], xt[j][:],
                    reads=[f"x{j}"], writes=[f"y{j}"])
        ctx.barrier()

    return p.close() if standalone else None



def build_stage_l0ffn(p=None, io=None):
    standalone = p is None
    if standalone:
        p = Prog("l0ffn").__enter__()
    nc, ctx = p.nc, p.ctx
    xT = io["xT"] if io else p.din("xT", [E, S], F32)
    wgT = p.din("wgT", [E, FH], I32)
    wuT = p.din("wuT", [E, FH], I32)
    wdT = p.din("wdT", [FH, E], I32)
    yT = io["yT"] if io else p.dout("yT", [E, S], F32)

    setup_common(p)
    PS = p.psum

    wg = p.sb("wg", [128, NE * FH], BF16)
    wu = p.sb("wu", [128, NE * FH], BF16)
    for j in range(NE):
        ctx.dma("pool", wg[:, j * FH:(j + 1) * FH], wgT[j * 128:(j + 1) * 128, :],
                writes=[f"wg{j}"])
        ctx.dma("pool", wu[:, j * FH:(j + 1) * FH], wuT[j * 128:(j + 1) * 128, :],
                writes=[f"wu{j}"])
    wdb = [p.sb(f"wdb{i}", [128, NF * 128], BF16) for i in range(3)]
    wdT_r = wdT.rearrange("(j p) e -> j p e", p=128)
    wd_bf = nc.dram_tensor("wd_bf", [NE, 128, NF * 128], BF16, kind="Internal").ap()
    for fo in range(NE):
        wb = wdb[fo % 3]
        wb_r = f"wdb{fo % 3}"
        ctx.dma("pool", wb[:].rearrange("p (j e) -> p j e", e=128),
                wdT_r[:, :, fo * 128:(fo + 1) * 128].rearrange("j p e -> p j e"),
                writes=[wb_r])
        ctx.dma("qs", wd_bf[fo], wb[:], reads=[wb_r], writes=[f"wdbf{fo}"])

    xt = [p.sb(f"xt{j}", [128, Tc], F32) for j in range(NE)]
    lnx = [p.sb(f"lnxx{j}", [128, Tc], BF16) for j in range(NE)]
    scr = make_ln_scratch(p)
    scr["psA"], scr["psB"], scr["psC"] = PS[4], PS[5], PS[6]
    scr["psA_r"], scr["psB_r"], scr["psC_r"] = "ps4", "ps5", "ps6"
    hb = p.sb("hb", [128, NF * Tc], BF16)
    gs = p.sb("gs", [128, Tc], BF16)
    us = p.sb("us", [128, Tc], BF16)
    obf = [p.sb(f"obf{j}", [128, Tc], BF16) for j in range(NE)]

    SC1 = np.float32(1.0 / 127.0)
    SC2 = np.float32(1.0 / (127.0 * 127.0))

    for c in range(NCH):
        t0 = c * Tc
        for j in range(NE):
            ctx.dma("qs", xt[j][:], xT[j * 128:(j + 1) * 128, t0:t0 + Tc],
                    writes=[f"x{j}"])
        emit_ln(p, f"L{c}_", xt, "x", lnx, "lnx", scr)

        for fo in range(NF):
            pg = PS[fo % 2]
            pg_r = f"ps{fo % 2}"
            pu = PS[2 + fo % 2]
            pu_r = f"ps{2 + fo % 2}"
            for j in range(NE):
                ctx.op("te", lambda t, fo=fo, j=j, pg=pg: t.matmul(
                    pg[:], wg[:, j * FH + fo * 128: j * FH + (fo + 1) * 128],
                    lnx[j][:], start=(j == 0), stop=(j == NE - 1)),
                    reads=[f"lnx{j}", f"wg{j}"], writes=[pg_r])
            for j in range(NE):
                ctx.op("te", lambda t, fo=fo, j=j, pu=pu: t.matmul(
                    pu[:], wu[:, j * FH + fo * 128: j * FH + (fo + 1) * 128],
                    lnx[j][:], start=(j == 0), stop=(j == NE - 1)),
                    reads=[f"lnx{j}", f"wu{j}"], writes=[pu_r])
            ctx.op("act", lambda s, pg=pg: s.activation(gs[:], pg[:], AF.Silu, scale=SC1),
                   reads=[pg_r], writes=["gs"])
            ctx.op("act", lambda s, pu=pu: s.copy(us[:], pu[:]),
                   reads=[pu_r], writes=["us"])
            ctx.op("dve", lambda v, fo=fo: v.tensor_mul(
                hb[:, fo * Tc:(fo + 1) * Tc], gs[:], us[:]),
                reads=["gs", "us"], writes=[f"hb{fo}"])

        for fo in range(NE):
            wb = wdb[fo % 3]
            wb_r = f"wdb{fo % 3}"
            ctx.dma("qs", wb[:], wd_bf[fo], reads=[f"wdbf{fo}"], writes=[wb_r])
            pd = PS[fo % 2]
            pd_r = f"ps{fo % 2}"
            for j in range(NF):
                ctx.op("te", lambda t, fo=fo, j=j, pd=pd, wb=wb: t.matmul(
                    pd[:], wb[:, j * 128:(j + 1) * 128],
                    hb[:, j * Tc:(j + 1) * Tc], start=(j == 0), stop=(j == NF - 1)),
                    reads=[f"hb{j}", wb_r], writes=[pd_r])
            ctx.op("act", lambda s, fo=fo, pd=pd: s.activation(
                obf[fo][:], pd[:], AF.Identity, scale=SC2),
                reads=[pd_r], writes=[f"obf{fo}"])

        emit_ln_residual(p, f"R{c}_", obf, "obf", xt, "x", scr)
        for j in range(NE):
            ctx.dma("qs", yT[j * 128:(j + 1) * 128, t0:t0 + Tc], xt[j][:],
                    reads=[f"x{j}"], writes=[f"y{j}"])
        ctx.barrier()
    return p.close() if standalone else None



def build_stage_l1mamba(dbg=False, p=None, io=None):
    standalone = p is None
    if standalone:
        p = Prog("l1mamba").__enter__()
    nc, ctx = p.nc, p.ctx
    xT = io["xT"] if io else p.din("xT", [E, S], F32)
    W_in = p.din("W_in", [E, 2 * DIN], F32)
    conv_wP = p.din("conv_wP", [128, ND * DCONV], F32)
    conv_bP = p.din("conv_bP", [128, ND], F32)
    W_x = p.din("W_x", [DIN, DTR + 2 * DST], F32)
    W_dt = p.din("W_dt", [DTR, DIN], F32)
    b_dtP = p.din("b_dtP", [128, ND], F32)
    A_logP = p.din("A_logP", [128, ND * DST], F32)
    D_P = p.din("D_P", [128, ND], F32)
    W_out = p.din("W_out", [DIN, E], F32)
    gammaP = p.din("gammaP", [128, NE], F32)
    betaP = p.din("betaP", [128, NE], F32)
    yT = io["yT"] if io else p.dout("yT", [E, S], F32)

    setup_common(p)
    PS = p.psum

    cw = p.sb("cw", [128, ND * DCONV], F32)
    cb = p.sb("cb", [128, ND], F32)
    bdt = p.sb("bdt", [128, ND], F32)
    alog = p.sb("alog", [128, ND * DST], F32)
    ascan = p.sb("ascan", [128, ND * DST], F32)
    dP = p.sb("dP", [128, ND], F32)
    gmb = p.sb("gmb", [128, NE], F32)
    btb = p.sb("btb", [128, NE], F32)
    wx = p.sb("wx", [128, ND * 96], BF16)
    wdt = p.sb("wdt", [64, DIN], BF16)
    sel = p.sb("sel", [16, 16 * 128], BF16)
    p.push_scope()
    sel_f = p.sb("sel_f", [16, 16 * 128], F32)
    ctx.dma("qs", cw[:], conv_wP, writes=["cw"])
    ctx.dma("qs", cb[:], conv_bP, writes=["cb"])
    ctx.dma("qs", bdt[:], b_dtP, writes=["bdt"])
    ctx.dma("qs", alog[:], A_logP, writes=["alog"])
    ctx.dma("qs", dP[:], D_P, writes=["dP"])
    ctx.dma("qs", gmb[:], gammaP, writes=["gmbt"])
    ctx.dma("qs", btb[:], betaP, writes=["gmbt"])
    ctx.dma("pool", wdt[:], W_dt, writes=["wdt"])
    for g in range(ND):
        ctx.dma("pool", wx[:, g * 96:(g + 1) * 96], W_x[g * 128:(g + 1) * 128, :],
                writes=["wx"])
    ctx.op("act", lambda s: s.activation(ascan[:], alog[:], AF.Exp),
           reads=["alog"], writes=["ascan"])
    ctx.op("dve", lambda v: v.tensor_scalar_mul(ascan[:], ascan[:], -1.0),
           reads=["ascan"], writes=["ascan"])
    ctx.op("pool", lambda g: g.memset(sel_f[:], 1.0), writes=["self"])
    ctx.op("pool", lambda g: g.affine_select(
        sel_f[:].rearrange("p (j c) -> p j c", c=128), sel_f[:].rearrange("p (j c) -> p j c", c=128),
        [[1, 16], [0, 128]], ALU.is_equal, 0.0, base=0, channel_multiplier=-1),
        reads=["self"], writes=["self"])
    ctx.op("dve", lambda v: v.tensor_copy(sel[:], sel_f[:]), reads=["self"], writes=["sel"])
    p.pop_scope()
    ctx.barrier()

    winb = [p.sb(f"winb{i}", [128, NE * 128], BF16) for i in range(2)]
    W_in_r = W_in.rearrange("(j p) d -> j p d", p=128)
    woutb = [p.sb(f"woutb{i}", [128, ND * 128], BF16) for i in range(2)]
    W_out_r = W_out.rearrange("(j p) e -> j p e", p=128)
    win_bf = nc.dram_tensor("win_bf", [2 * ND, 128, NE * 128], BF16,
                            kind="Internal").ap()
    wout_bf = nc.dram_tensor("wout_bf", [NE, 128, ND * 128], BF16,
                             kind="Internal").ap()
    for g in range(2 * ND):
        wb = winb[g % 2]
        wb_r = f"winb{g % 2}"
        ctx.dma("pool", wb[:].rearrange("p (j d) -> p j d", d=128),
                W_in_r[:, :, g * 128:(g + 1) * 128].rearrange("j p d -> p j d"),
                writes=[wb_r])
        ctx.dma("qs", win_bf[g], wb[:], reads=[wb_r], writes=[f"winbf{g}"])
    for fo in range(NE):
        wb = woutb[fo % 2]
        wb_r = f"woutb{fo % 2}"
        ctx.dma("pool", wb[:].rearrange("p (j e) -> p j e", e=128),
                W_out_r[:, :, fo * 128:(fo + 1) * 128].rearrange("j p e -> p j e"),
                writes=[wb_r])
        ctx.dma("qs", wout_bf[fo], wb[:], reads=[wb_r], writes=[f"woutbf{fo}"])

    xt = [p.sb(f"xt{j}", [128, Tc], F32) for j in range(NE)]
    lnx = [p.sb(f"lnxx{j}", [128, Tc], BF16) for j in range(NE)]
    scr = make_ln_scratch(p, no_xbf=True)
    scr["psA"], scr["psB"], scr["psC"] = PS[4], PS[5], PS[6]
    scr["psA_r"], scr["psB_r"], scr["psC_r"] = "ps4", "ps5", "ps6"
    xin = p.sb("xin", [128, ND * (Tc + 3)], BF16)
    xc = p.sb("xc", [128, ND * Tc], BF16)
    zrec = [p.sb("zrec0", [128, Tc], BF16)] * 2
    dtb = p.sb("dtb", [64, Tc], BF16)
    bcf = p.sb("bcf", [16, 2 * Tc], BF16)
    delta = p.sb("delta", [128, ND * Tc], BF16)
    du = p.sb("du", [128, ND * Tc], BF16)
    conv_acc = p.sb("conv_acc", [128, Tc], F32)
    conv_accb = p.sb("conv_accb", [128, Tc], BF16)
    e_scr = conv_acc
    state = p.sb("state", [128, ND * DST], F32)
    a_t = [p.sb("a_t0", [128, Tc], F32), p.sb("a_t1", [128, Tc], F32)]
    b_stk = p.sb("b_stk", [128, DST * Tc], BF16)
    h_st = p.sb("h_st", [128, DST * Tc], BF16)
    hc_st = h_st
    B_st = p.sb("B_st", [128, DST * Tc], BF16)
    C_st = p.sb("C_st", [128, DST * Tc], BF16)
    yz = du
    mo = [lnx[j] for j in range(NE)]
    h1 = [delta[:, j * Tc:(j + 1) * Tc] for j in range(NE)]
    gout = [xc[:, j * Tc:(j + 1) * Tc] for j in range(NE)]

    st_all = [f"st{g}_{si}" for g in range(ND) for si in range(DST)]
    dbf = conv_acc
    ctx.op("pool", lambda g: g.memset(state[:], 0.0), writes=st_all)
    ctx.op("pool", lambda g: g.memset(xin[:], 0.0), writes=["xin_halo"])

    for c in range(NCH):
        t0 = c * Tc
        for j in range(NE):
            ctx.dma("qs", xt[j][:], xT[j * 128:(j + 1) * 128, t0:t0 + Tc],
                    writes=[f"x{j}"])
        emit_ln(p, f"L{c}_", xt, "x", lnx, "lnx", scr, xbf_is_out=True)

        if c > 0:
            ctx.op("dve", lambda v: v.tensor_copy(
                xin[:].rearrange("p (g w) -> p g w", g=ND)[:, :, 0:3],
                xin[:].rearrange("p (g w) -> p g w", g=ND)[:, :, Tc:Tc + 3]),
                reads=[f"xin{gg}" for gg in range(ND)], writes=["xin_halo"])

        for g in range(ND):
            wb = winb[g % 2]
            wb_r = f"winb{g % 2}"
            ctx.dma("qs", wb[:], win_bf[g], reads=[f"winbf{g}"], writes=[wb_r])
            pb = PS[g % 2]
            pr = f"ps{g % 2}"
            for j in range(NE):
                ctx.op("te", lambda t, g=g, j=j, pb=pb, wb=wb: t.matmul(
                    pb[:], wb[:, j * 128:(j + 1) * 128], lnx[j][:],
                    start=(j == 0), stop=(j == NE - 1)),
                    reads=[f"lnx{j}", wb_r], writes=[pr])
            ctx.op("act", lambda s, g=g, pb=pb: s.copy(
                xin[:, g * (Tc + 3) + 3: (g + 1) * (Tc + 3)], pb[:]),
                reads=[pr], writes=[f"xin{g}"], tag=f"xinev{g}")

        for g in range(ND):
            xg = xin[:, g * (Tc + 3): (g + 1) * (Tc + 3)]
            ctx.op("dve", lambda v, g=g, xg=xg: v.tensor_scalar(
                conv_accb[:], xg[:, 0:Tc], cw[:, g * DCONV: g * DCONV + 1],
                cb[:, g:g + 1], ALU.mult, ALU.add),
                reads=[f"xin{g}", "xin_halo", "cw", "cb"], writes=["conv_accb"])
            for kk in range(1, DCONV):
                ctx.op("dve", lambda v, g=g, kk=kk, xg=xg: v.scalar_tensor_tensor(
                    conv_accb[:], xg[:, kk:kk + Tc], cw[:, g * DCONV + kk: g * DCONV + kk + 1],
                    conv_accb[:], ALU.mult, ALU.add),
                    reads=[f"xin{g}", "xin_halo", "cw", "conv_accb"], writes=["conv_accb"])
            ctx.op("act", lambda s, g=g: s.activation(
                xc[:, g * Tc:(g + 1) * Tc], conv_accb[:], AF.Silu),
                reads=["conv_accb"], writes=["xc"])

        for (lo, hi, pbidx, nm) in ((0, 64, 2, "dt"), (64, 80, 3, "B"), (80, 96, 3, "C")):
            pb = PS[pbidx]
            pr = f"ps{pbidx}"
            if nm == "C":
                pb = PS[2]
                pr = "ps2"
            for g in range(ND):
                ctx.op("te", lambda t, g=g, lo=lo, hi=hi, pb=pb: t.matmul(
                    pb[0:hi - lo, :], wx[:, g * 96 + lo: g * 96 + hi],
                    xc[:, g * Tc:(g + 1) * Tc],
                    start=(g == 0), stop=(g == ND - 1)),
                    reads=["xc", "wx"], writes=[pr])
            if nm == "dt":
                ctx.op("act", lambda s, pb=pb: s.copy(dtb[:], pb[0:64, :]),
                       reads=[pr], writes=["dtb"])
            elif nm == "B":
                ctx.op("act", lambda s, pb=pb: s.copy(bcf[:, 0:Tc], pb[0:16, :]),
                       reads=[pr], writes=["bcf"])
            else:
                ctx.op("act", lambda s, pb=pb: s.copy(bcf[:, Tc:2 * Tc], pb[0:16, :]),
                       reads=[pr], writes=["bcf"])

        for g in range(ND):
            pb = PS[2 + g % 2]
            pr = f"ps{2 + g % 2}"
            ctx.op("te", lambda t, g=g, pb=pb: t.matmul(
                pb[:], wdt[:, g * 128:(g + 1) * 128], dtb[:],
                start=True, stop=True),
                reads=["dtb", "wdt"], writes=[pr])
            ctx.op("act", lambda s, g=g, pb=pb: s.activation(
                e_scr[:], pb[:], AF.Exp, bias=bdt[:, g:g + 1]),
                reads=[pr, "bdt"], writes=["conv_acc"])
            ctx.op("act", lambda s, g=g: s.activation(
                delta[:, g * Tc:(g + 1) * Tc], e_scr[:], AF.Ln, bias=np.float32(1.0)),
                reads=["conv_acc"], writes=[f"dl{g}"])
        for g in range(ND):
            ctx.op("dve", lambda v, g=g: v.tensor_mul(
                du[:, g * Tc:(g + 1) * Tc], delta[:, g * Tc:(g + 1) * Tc],
                xc[:, g * Tc:(g + 1) * Tc]),
                reads=[f"dl{g}", "xc", "yz"], writes=[f"du{g}"])

        for si in range(DST):
            pB = PS[2]
            pC = PS[3]
            ctx.op("te", lambda t, si=si, pB=pB: t.matmul(
                pB[:], sel[:, si * 128:(si + 1) * 128], bcf[:, 0:Tc],
                start=True, stop=True),
                reads=["bcf", "sel"], writes=["ps2"])
            ctx.op("act", lambda s, si=si, pB=pB: s.copy(
                B_st[:, si * Tc:(si + 1) * Tc], pB[:]),
                reads=["ps2"], writes=["Bst"])
            ctx.op("te", lambda t, si=si, pC=pC: t.matmul(
                pC[:], sel[:, si * 128:(si + 1) * 128], bcf[:, Tc:2 * Tc],
                start=True, stop=True),
                reads=["bcf", "sel"], writes=["ps3"])
            ctx.op("act", lambda s, si=si, pC=pC: s.copy(
                C_st[:, si * Tc:(si + 1) * Tc], pC[:]),
                reads=["ps3"], writes=["Cst"])
        for g in range(ND):
            dug = du[:, g * Tc:(g + 1) * Tc]
            dug_b = bass.AP(dug.tensor, dug.offset,
                            [[ND * Tc, 128], [0, DST], [1, Tc]])
            ctx.op("dve", lambda v, g=g, dug_b=dug_b: v.tensor_mul(
                b_stk[:].rearrange("p (s t) -> p s t", t=Tc), dug_b,
                B_st[:].rearrange("p (s t) -> p s t", t=Tc)),
                reads=[f"du{g}", "Bst"], writes=["bstk"])
            for si in range(DST):
                ib = si % 2
                av = a_t[ib]
                st_res = f"st{g}_{si}"
                ctx.op("act", lambda s, g=g, si=si, av=av: s.activation(
                    av[:], delta[:, g * Tc:(g + 1) * Tc], AF.Exp,
                    scale=ascan[:, g * DST + si: g * DST + si + 1]),
                    reads=[f"dl{g}"], sreads=["ascan"], writes=[f"a{ib}"])
                init_arg = 0.0 if c == 0 else state[:, g * DST + si: g * DST + si + 1]
                ctx.op("dve", lambda v, g=g, si=si, av=av, init_arg=init_arg:
                       v.tensor_tensor_scan(
                           h_st[:, si * Tc:(si + 1) * Tc], av[:],
                           b_stk[:, si * Tc:(si + 1) * Tc], init_arg,
                           ALU.mult, ALU.add),
                       reads=[f"a{ib}", "bstk"], sreads=([st_res] if c > 0 else []),
                       writes=["hst"], late=["hst"])
            ctx.op("dve", lambda v, g=g: v.tensor_copy(
                state[:, g * DST:(g + 1) * DST],
                h_st[:].rearrange("p (s t) -> p s t", t=Tc)[:, :, Tc - 1:Tc]
                .rearrange("p s t -> p (s t)")),
                reads=["hst"], writes=[f"st{g}_{si2}" for si2 in range(DST)],
                late=[f"st{g}_{si2}" for si2 in range(DST)])
            ctx.op("dve", lambda v, g=g: v.tensor_mul(hc_st[:], h_st[:], C_st[:]),
                   reads=["hst", "Cst"], writes=["hst"])
            w = DST * Tc
            while w > Tc:
                hw = w // 2
                ctx.op("dve", lambda v, hw=hw, w=w: v.tensor_add(
                    hc_st[:, 0:hw], hc_st[:, 0:hw], hc_st[:, hw:w]),
                    reads=["hst"], writes=["hst"])
                w = hw
            wb = winb[g % 2]
            wb_r = f"winb{g % 2}"
            ctx.dma("qs", wb[:], win_bf[ND + g], reads=[f"winbf{ND + g}"],
                    writes=[wb_r])
            pz = PS[4 + g % 2]
            pz_r = f"ps{4 + g % 2}"
            for j in range(NE):
                ctx.op("te", lambda t, g=g, j=j, pz=pz, wb=wb: t.matmul(
                    pz[:], wb[:, j * 128:(j + 1) * 128], lnx[j][:],
                    start=(j == 0), stop=(j == NE - 1)),
                    reads=[f"lnx{j}", wb_r], writes=[pz_r])
            zb = zrec[g % 2]
            ctx.op("act", lambda s, g=g, pz=pz, zb=zb: s.activation(
                zb[:], pz[:], AF.Silu),
                reads=[pz_r], writes=["zr0"])
            ctx.op("dve", lambda v, g=g: v.scalar_tensor_tensor(
                hc_st[:, 0:Tc], xc[:, g * Tc:(g + 1) * Tc],
                dP[:, g:g + 1], hc_st[:, 0:Tc], ALU.mult, ALU.add),
                reads=["xc", "hst"], sreads=["dP"], writes=["hst"])
            ctx.op("dve", lambda v, g=g, zb=zb: v.tensor_mul(
                yz[:, g * Tc:(g + 1) * Tc], hc_st[:, 0:Tc], zb[:]),
                reads=["hst", "zr0", f"du{g}"], writes=["yz", f"du{g}"])

        for fo in range(NE):
            wb = woutb[fo % 2]
            wb_r = f"woutb{fo % 2}"
            ctx.dma("qs", wb[:], wout_bf[fo], reads=[f"woutbf{fo}"], writes=[wb_r])
            pb = PS[fo % 2]
            pr = f"ps{fo % 2}"
            for g in range(ND):
                ctx.op("te", lambda t, fo=fo, g=g, pb=pb, wb=wb: t.matmul(
                    pb[:], wb[:, g * 128:(g + 1) * 128], yz[:, g * Tc:(g + 1) * Tc],
                    start=(g == 0), stop=(g == ND - 1)),
                    reads=["yz"] + [wb_r], writes=[pr])
            ctx.op("act", lambda s, fo=fo, pb=pb: s.copy(mo[fo][:], pb[:]),
                   reads=[pr], writes=[f"mo{fo}", f"lnx{fo}"])
            ctx.op("dve", lambda v, fo=fo: v.tensor_add(h1[fo][:], xt[fo][:], mo[fo][:]),
                   reads=[f"mo{fo}", f"x{fo}"], writes=[f"h1{fo}"])
        emit_ln(p, f"G{c}_", h1, "h1", gout, "gout", scr, gamma=gmb, beta=btb,
                stats_from=None, xbf_is_out=True)
        emit_ln_residual(p, f"R{c}_", gout, "gout", xt, "x", scr)
        for j in range(NE):
            ctx.dma("qs", yT[j * 128:(j + 1) * 128, t0:t0 + Tc], xt[j][:],
                    reads=[f"x{j}"], writes=[f"y{j}"])
        if c == NCH - 1:
            ctx.barrier()
    return p.close() if standalone else None



def build_stage_l1ffn(p=None, io=None):
    standalone = p is None
    if standalone:
        p = Prog("l1ffn").__enter__()
    nc, ctx = p.nc, p.ctx
    xT = io["xT"] if io else p.din("xT", [E, S], F32)
    WrT = p.din("WrT", [E, E], F32)
    WkT = p.din("WkT", [E, E], F32)
    WvT = p.din("WvT", [E, E], F32)
    mu_rP = p.din("mu_rP", [128, NE], F32)
    mu_kP = p.din("mu_kP", [128, NE], F32)
    yT = io["yT"] if io else p.dout("yT", [E, S], F32)

    setup_common(p)
    PS = p.psum
    wr = p.sb("wr", [128, NE * E], BF16)
    wk = p.sb("wk", [128, NE * E], BF16)
    wv = p.sb("wv", [128, NE * E], BF16)
    mur = p.sb("mur", [128, NE], F32)
    muk = p.sb("muk", [128, NE], F32)
    for j in range(NE):
        ctx.dma("pool", wr[:, j * E:(j + 1) * E], WrT[j * 128:(j + 1) * 128, :],
                writes=[f"wr{j}"])
        ctx.dma("pool", wk[:, j * E:(j + 1) * E], WkT[j * 128:(j + 1) * 128, :],
                writes=[f"wk{j}"])
        ctx.dma("pool", wv[:, j * E:(j + 1) * E], WvT[j * 128:(j + 1) * 128, :],
                writes=[f"wv{j}"])
    ctx.dma("qs", mur[:], mu_rP, writes=["mur"])
    ctx.dma("qs", muk[:], mu_kP, writes=["muk"])

    T1 = Tc + 1
    xt = [p.sb(f"xt{j}", [128, Tc], F32) for j in range(NE)]
    lnx = [p.sb(f"lnxx{j}", [128, T1], BF16) for j in range(NE)]
    scr = make_ln_scratch(p)
    scr["psA"], scr["psB"], scr["psC"] = PS[4], PS[5], PS[6]
    scr["psA_r"], scr["psB_r"], scr["psC_r"] = "ps4", "ps5", "ps6"
    dsh = p.sb("dsh", [128, Tc], BF16)
    xr = [p.sb(f"xr{j}", [128, Tc], BF16) for j in range(NE)]
    xk = [p.sb(f"xk{j}", [128, Tc], BF16) for j in range(NE)]
    rb = [p.sb(f"rbs{j}", [128, Tc], BF16) for j in range(NE)]
    k2 = [p.sb(f"k2{j}", [128, Tc], BF16) for j in range(NE)]
    krelu = p.sb("krelu", [128, Tc], BF16)
    obf = [p.sb(f"obf{j}", [128, Tc], BF16) for j in range(NE)]

    lnx_out = [lnx[j][:, 1:T1] for j in range(NE)]
    for c in range(NCH):
        t0 = c * Tc
        for j in range(NE):
            if c == 0:
                ctx.op("pool", lambda g, j=j: g.memset(lnx[j][:, 0:1], 0.0),
                       writes=[f"lnp{j}"])
            else:
                ctx.op("dve", lambda v, j=j: v.tensor_copy(lnx[j][:, 0:1],
                                                           lnx[j][:, Tc:T1]),
                       reads=[f"lnx{j}"], writes=[f"lnp{j}"])
            ctx.dma("qs", xt[j][:], xT[j * 128:(j + 1) * 128, t0:t0 + Tc],
                    writes=[f"x{j}"])
        emit_ln(p, f"L{c}_", xt, "x", lnx_out, "lnx", scr)

        for j in range(NE):
            ctx.op("dve", lambda v, j=j: v.tensor_sub(dsh[:], lnx[j][:, 1:T1],
                                                      lnx[j][:, 0:Tc]),
                   reads=[f"lnx{j}", f"lnp{j}"], writes=["dsh"])
            ctx.op("dve", lambda v, j=j: v.scalar_tensor_tensor(
                xr[j][:], dsh[:], mur[:, j:j + 1], lnx[j][:, 0:Tc], ALU.mult, ALU.add),
                reads=["dsh", f"lnx{j}"], sreads=["mur"], writes=[f"xr{j}"])
            ctx.op("dve", lambda v, j=j: v.scalar_tensor_tensor(
                xk[j][:], dsh[:], muk[:, j:j + 1], lnx[j][:, 0:Tc], ALU.mult, ALU.add),
                reads=["dsh", f"lnx{j}"], sreads=["muk"], writes=[f"xk{j}"])

        for fo in range(NE):
            pb = PS[fo % 2]
            pr = f"ps{fo % 2}"
            for j in range(NE):
                ctx.op("te", lambda t, fo=fo, j=j, pb=pb: t.matmul(
                    pb[:, 0:Tc], wr[:, j * E + fo * 128: j * E + (fo + 1) * 128],
                    xr[j][:], start=(j == 0), stop=(j == NE - 1)),
                    reads=[f"xr{j}", f"wr{j}"], writes=[pr])
            ctx.op("act", lambda s, fo=fo, pb=pb: s.activation(
                rb[fo][:], pb[:, 0:Tc], AF.Sigmoid),
                reads=[pr], writes=[f"rb{fo}"])
        for fo in range(NE):
            pb = PS[2 + fo % 2]
            pr = f"ps{2 + fo % 2}"
            for j in range(NE):
                ctx.op("te", lambda t, fo=fo, j=j, pb=pb: t.matmul(
                    pb[:, 0:Tc], wk[:, j * E + fo * 128: j * E + (fo + 1) * 128],
                    xk[j][:], start=(j == 0), stop=(j == NE - 1)),
                    reads=[f"xk{j}", f"wk{j}"], writes=[pr])
            ctx.op("act", lambda s, fo=fo, pb=pb: s.activation(
                krelu[:], pb[:, 0:Tc], AF.Relu),
                reads=[pr], writes=["krelu"])
            ctx.op("act", lambda s, fo=fo: s.activation(
                k2[fo][:], krelu[:], AF.Square),
                reads=["krelu"], writes=[f"k2{fo}"])
        for fo in range(NE):
            pb = PS[fo % 2]
            pr = f"ps{fo % 2}"
            for j in range(NE):
                ctx.op("te", lambda t, fo=fo, j=j, pb=pb: t.matmul(
                    pb[:, 0:Tc], wv[:, j * E + fo * 128: j * E + (fo + 1) * 128],
                    k2[j][:], start=(j == 0), stop=(j == NE - 1)),
                    reads=[f"k2{j}", f"wv{j}"], writes=[pr])
            ctx.op("dve", lambda v, fo=fo, pb=pb: v.tensor_mul(
                obf[fo][:], pb[:, 0:Tc], rb[fo][:]),
                reads=[pr, f"rb{fo}"], writes=[f"obf{fo}"])

        emit_ln_residual(p, f"R{c}_", obf, "obf", xt, "x", scr)
        for j in range(NE):
            ctx.dma("qs", yT[j * 128:(j + 1) * 128, t0:t0 + Tc], xt[j][:],
                    reads=[f"x{j}"], writes=[f"y{j}"])
        ctx.barrier()
    return p.close() if standalone else None


def build_fused():
    p = Prog("fused").__enter__()
    nc = p.nc
    xT = p.din("xT", [E, S], F32)
    x1 = nc.dram_tensor("x1", [E, S], F32, kind="Internal").ap()
    x2 = nc.dram_tensor("x2", [E, S], F32, kind="Internal").ap()
    x3 = nc.dram_tensor("x3", [E, S], F32, kind="Internal").ap()
    yT = p.dout("yT", [E, S], F32)
    setup_common(p)
    hops = [xT, x1, x2, x3, yT]
    fns = [build_stage_l0attn, build_stage_l0ffn, build_stage_l1mamba,
           build_stage_l1ffn]
    for k, fn in enumerate(fns):
        p.push_scope()
        fn(p=p, io={"xT": hops[k], "yT": hops[k + 1]})
        p.pop_scope()
    return p.close()



def _prep_l0attn(inputs):
    w_in = inputs["L0_attn_w_in"]
    b_in = inputs["L0_attn_b_in"]
    w_out = inputs["L0_attn_w_out"]
    b_out = inputs["L0_attn_b_out"]
    return {
        "w_qkT": np.ascontiguousarray(w_in[:2048].T),
        "w_vT": np.ascontiguousarray(w_in[2048:].T),
        "b_inP": np.ascontiguousarray(b_in.reshape(24, 128).T),
        "w_oT": np.ascontiguousarray(w_out.T),
        "b_oP": np.ascontiguousarray(b_out.reshape(8, 128).T),
    }


def _prep_l0ffn(inputs):
    return {
        "wgT": np.ascontiguousarray(inputs["L0_ffn_w_gate"].T),
        "wuT": np.ascontiguousarray(inputs["L0_ffn_w_up"].T),
        "wdT": np.ascontiguousarray(inputs["L0_ffn_w_down"].T),
    }


def _perpart(v, n):
    return np.ascontiguousarray(np.asarray(v).reshape(n, 128).T)


def _prep_l1mamba(inputs):
    A_log = np.asarray(inputs["L1_attn_A_log"])
    conv_w = np.asarray(inputs["L1_attn_conv_w"])
    return {
        "W_in": np.ascontiguousarray(inputs["L1_attn_W_in"]),
        "conv_wP": np.ascontiguousarray(conv_w.reshape(16, 128, 4).transpose(1, 0, 2)
                                        .reshape(128, 64)),
        "conv_bP": _perpart(inputs["L1_attn_conv_b"], 16),
        "W_x": np.ascontiguousarray(inputs["L1_attn_W_x"]),
        "W_dt": np.ascontiguousarray(inputs["L1_attn_W_dt"]),
        "b_dtP": _perpart(inputs["L1_attn_b_dt"], 16),
        "A_logP": np.ascontiguousarray(A_log.reshape(16, 128, 16).transpose(1, 0, 2)
                                       .reshape(128, 256)),
        "D_P": _perpart(inputs["L1_attn_D"], 16),
        "W_out": np.ascontiguousarray(inputs["L1_attn_W_out"]),
        "gammaP": _perpart(inputs["L1_attn_gamma"], 8),
        "betaP": _perpart(inputs["L1_attn_beta"], 8),
    }


def _prep_l1ffn(inputs):
    return {
        "WrT": np.ascontiguousarray(inputs["L1_ffn_W_r"].T),
        "WkT": np.ascontiguousarray(inputs["L1_ffn_W_k"].T),
        "WvT": np.ascontiguousarray(inputs["L1_ffn_W_v"].T),
        "mu_rP": _perpart(inputs["L1_ffn_mu_r"], 8),
        "mu_kP": _perpart(inputs["L1_ffn_mu_k"], 8),
    }


STAGES = [
    (build_stage_l0attn, _prep_l0attn),
    (build_stage_l0ffn, _prep_l0ffn),
    (build_stage_l1mamba, _prep_l1mamba),
    (build_stage_l1ffn, _prep_l1ffn),
]


FUSED = True


def kernel(**inputs):
    x = inputs["L0_attn_x"]
    xT = [np.ascontiguousarray(x[b].T) for b in range(NCORE)]

    if FUSED:
        common = {}
        for _, prep in STAGES:
            common.update(prep(inputs))
        nc = build_fused()
        maps = [dict(xT=xT[b], **common) for b in range(NCORE)]
        res = run_bass_kernel_spmd(nc, maps, core_ids=list(range(NCORE)))
        xT = [res.results[b]["yT"] for b in range(NCORE)]
    else:
        for build, prep in STAGES:
            common = prep(inputs)
            nc = build()
            maps = [dict(xT=xT[b], **common) for b in range(NCORE)]
            res = run_bass_kernel_spmd(nc, maps, core_ids=list(range(NCORE)))
            xT = [np.ascontiguousarray(res.results[b]["yT"]) for b in range(NCORE)]

    out = np.stack([xT[b].T for b in range(NCORE)], axis=0).astype(np.float32)
    return out


# version 12
# speedup vs baseline: 1.1840x; 1.1840x over previous
import contextlib
import numpy as np
import concourse.bass as bass
import concourse.mybir as mybir
from concourse.bass_utils import run_bass_kernel_spmd

F32 = mybir.dt.float32
BF16 = mybir.dt.bfloat16
I32 = mybir.dt.int32
AF = mybir.ActivationFunctionType
ALU = mybir.AluOpType
AX = mybir.AxisListType

S = 4096
E = 1024
NE = E // 128
H = 16
HD = 64
WIN = 128
FH = 3072
NF = FH // 128
DIN = 2048
ND = DIN // 128
DST = 16
DTR = 64
DCONV = 4
LN_EPS = 1e-5
QFS = 1.0 / np.sqrt(np.float32(HD))

NCORE = 8
Tc = 512
NCH = S // Tc
QT = Tc // 128

ENGINES = ("te", "dve", "act", "pool", "qs")
DMA_STREAMS = ("qs", "pool")
NDMASEM = 8


class Op:
    __slots__ = ("eng", "fn", "waits", "dma", "dma_cnt", "dma_sem_idx",
                 "dma_sem_use", "tag")

    def __init__(self, eng, fn, dma=False, tag=""):
        self.eng = eng
        self.fn = fn
        self.waits = {}
        self.dma = dma
        self.dma_cnt = 0
        self.tag = tag


class Ctx:

    def __init__(self, nc):
        self.nc = nc
        self.ops = {e: [] for e in ENGINES}
        self.res = {}
        self.dma_count = {e: 0 for e in DMA_STREAMS}

    def _r(self, name):
        return self.res.setdefault(name, {"writer": None, "readers": [], "late": False})

    def _add_wait(self, op, tok, force_same=False):
        kind, eng, cnt = tok
        if kind == "op" and eng == op.eng and not force_same:
            return
        key = (kind, eng)
        if op.waits.get(key, -1) < cnt:
            op.waits[key] = cnt

    def _pre(self, op, reads, writes, sreads=()):
        for nm in reads:
            r = self._r(nm)
            if r["writer"]:
                self._add_wait(op, r["writer"], force_same=r["late"])
        for nm in sreads:
            r = self._r(nm)
            if r["writer"]:
                self._add_wait(op, r["writer"], force_same=True)
        for nm in writes:
            r = self._r(nm)
            if r["writer"]:
                self._add_wait(op, r["writer"])
            for tok in r["readers"]:
                self._add_wait(op, tok)

    def _post(self, tok, reads, writes):
        for nm in reads:
            self._r(nm)["readers"].append(tok)
        for nm in writes:
            r = self._r(nm)
            r["writer"] = tok
            r["readers"] = []

    def op(self, eng, fn, reads=(), writes=(), sreads=(), late=(), tag=""):
        o = Op(eng, fn, tag=tag)
        self._pre(o, reads, writes, sreads=sreads)
        self.ops[eng].append(o)
        tok = ("op", eng, len(self.ops[eng]))
        self._post(tok, tuple(reads) + tuple(sreads), writes)
        for nm in late:
            self._r(nm)["late"] = True
        return o

    def barrier(self):
        snap_ops = {e: len(self.ops[e]) for e in ENGINES}
        snap_dma = {e: self.dma_count[e] for e in DMA_STREAMS}
        for e in ENGINES:
            o = Op(e, lambda h: h.nop(), tag="barrier")
            for e2 in ENGINES:
                if e2 != e and snap_ops[e2]:
                    o.waits[("op", e2)] = snap_ops[e2]
            for e2 in DMA_STREAMS:
                n = snap_dma[e2]
                for sidx in range(min(n, NDMASEM)):
                    uses = (n - 1 - sidx) // NDMASEM + 1
                    o.waits[("dma", (e2, sidx))] = 16 * uses
            self.ops[e].append(o)

    def dma(self, eng, out, in_, reads=(), writes=(), tag=""):
        assert eng in DMA_STREAMS
        o = Op(eng, None, dma=True, tag=tag)
        self._pre(o, reads, writes)
        self.dma_count[eng] += 1
        idx = (self.dma_count[eng] - 1) % NDMASEM
        use = (self.dma_count[eng] - 1) // NDMASEM + 1
        o.dma_cnt = self.dma_count[eng]
        o.dma_sem_idx = idx
        o.dma_sem_use = use

        def fn(h, _o=out, _i=in_):
            return h.dma_start(out=_o, in_=_i)
        o.fn = fn
        self.ops[eng].append(o)
        tok = ("dma", (eng, idx), 16 * use)
        self._post(tok, reads, writes)
        return o

    def finalize(self, block, sems, dma_sems):
        import bisect as _bisect
        nondma = {e: [i + 1 for i, o in enumerate(self.ops[e]) if not o.dma]
                  for e in ENGINES}

        def snap_idx(eng, cnt):
            lst = nondma[eng]
            k = _bisect.bisect_right(lst, cnt)
            return lst[k - 1] if k else 0

        for e in ENGINES:
            for o in self.ops[e]:
                newwaits = {}
                for (kind, eng), cnt in o.waits.items():
                    if kind == "op":
                        cnt = snap_idx(eng, cnt)
                        if cnt == 0:
                            continue
                    newwaits[(kind, eng)] = max(newwaits.get((kind, eng), -1), cnt)
                o.waits = newwaits
        need_inc = {e: set() for e in ENGINES}
        for e in ENGINES:
            for o in self.ops[e]:
                for (kind, eng), cnt in o.waits.items():
                    if kind == "op":
                        need_inc[eng].add(cnt)
        for e in ENGINES:
            if nondma[e]:
                need_inc[e].add(nondma[e][-1])

        handles = {"te": self.nc.tensor, "dve": self.nc.vector,
                   "act": self.nc.scalar, "pool": self.nc.gpsimd,
                   "qs": self.nc.sync}

        import bisect
        inc_sorted = {e: sorted(need_inc[e]) for e in ENGINES}

        def sem_val(eng, idx):
            return bisect.bisect_right(inc_sorted[eng], idx)

        def emit(e):
            h = handles[e]
            waited = {}
            for i, o in enumerate(self.ops[e]):
                if o.dma and o.dma_sem_use > 1:
                    key = ("dma", (e, o.dma_sem_idx))
                    prev = 16 * (o.dma_sem_use - 1)
                    if waited.get(key, -1) < prev:
                        h.wait_ge(dma_sems[e][o.dma_sem_idx], prev)
                        waited[key] = prev
                for key in sorted(o.waits, key=str):
                    cnt = o.waits[key]
                    if waited.get(key, -1) >= cnt:
                        continue
                    kind, eng = key
                    if kind == "op":
                        h.wait_ge(sems[eng], sem_val(eng, cnt))
                    else:
                        sename, sidx = eng
                        h.wait_ge(dma_sems[sename][sidx], cnt)
                    waited[key] = cnt
                inst = o.fn(h)
                if o.dma:
                    inst.then_inc(dma_sems[e][o.dma_sem_idx], 16)
                elif (i + 1) in need_inc[e]:
                    inst.then_inc(sems[e], 1)
            if e in DMA_STREAMS and self.dma_count[e]:
                n = self.dma_count[e]
                for sidx in range(min(n, NDMASEM)):
                    uses = (n - 1 - sidx) // NDMASEM + 1
                    h.wait_ge(dma_sems[e][sidx], 16 * uses)

        @block.tensor
        def _(te):
            emit("te")

        @block.vector
        def _(v):
            emit("dve")

        @block.scalar
        def _(s):
            emit("act")

        @block.gpsimd
        def _(g):
            emit("pool")

        @block.sync
        def _(sy):
            emit("qs")


class Prog:

    def __init__(self, name="prog"):
        self.nc = bass.Bass("TRN2", target_bir_lowering=False)
        self.stack = contextlib.ExitStack()
        self.ctx = Ctx(self.nc)
        self.psum = []
        self._psum_bf16 = None

    def __enter__(self):
        self.stack.__enter__()
        self.psum = [self.stack.enter_context(
            self.nc.psum_tensor(f"pb{i}", [128, 512], F32)) for i in range(7)]
        self._psum_bf16 = self.stack.enter_context(
            self.nc.psum_tensor("pb7", [128, 1024], BF16))
        self.sems = {e: self.stack.enter_context(self.nc.semaphore(f"sem_{e}"))
                     for e in ENGINES}
        self.dma_sems = {e: [self.stack.enter_context(self.nc.semaphore(f"dsem_{e}{i}"))
                             for i in range(NDMASEM)]
                         for e in DMA_STREAMS}
        return self

    def sb(self, name, shape, dt):
        st = getattr(self, "scopes", None)
        tgt = st[-1] if st else self.stack
        self._uniq = getattr(self, "_uniq", 0) + 1
        return tgt.enter_context(self.nc.sbuf_tensor(f"{name}_u{self._uniq}", shape, dt))

    def push_scope(self):
        if not hasattr(self, "scopes"):
            self.scopes = []
        self.scopes.append(contextlib.ExitStack())
        self.scopes[-1].__enter__()

    def pop_scope(self):
        self.scopes.pop().__exit__(None, None, None)

    def din(self, name, shape, dt):
        return self.nc.dram_tensor(name, shape, dt, kind="ExternalInput").ap()

    def dout(self, name, shape, dt):
        return self.nc.dram_tensor(name, shape, dt, kind="ExternalOutput").ap()

    def close(self):
        block = self.stack.enter_context(self.nc.Block())
        self.ctx.finalize(block, self.sems, self.dma_sems)
        self.stack.__exit__(None, None, None)
        return self.nc



def constf(p, v):
    if v not in p.consts:
        t = p.sb(f"cst{len(p.consts)}", [128, 1], F32)
        p.ctx.op("pool", lambda g, t=t, v=v: g.memset(t[:], float(v)), writes=["consts"])
        p.consts[v] = t
    return p.consts[v]


def setup_common(p):
    if hasattr(p, "ones"):
        return
    ctx = p.ctx
    p.ones = p.sb("ones", [128, 128], BF16)
    ctx.op("pool", lambda g: g.memset(p.ones[:], 1.0), writes=["ones"])
    p.ones_f = p.sb("ones_f", [1, 128], F32)
    ctx.op("pool", lambda g: g.memset(p.ones_f[:], 1.0), writes=["ones"])
    p.consts = {}
    p.ident_f = p.sb("ident_f", [128, 128], F32)
    p.ident = p.sb("ident", [128, 128], BF16)
    ctx.op("pool", lambda g: g.memset(p.ident_f[:], 1.0), writes=["identf"])
    ctx.op("pool", lambda g: g.affine_select(p.ident_f[:], p.ident_f[:], [[1, 128]],
                                             ALU.is_equal, 0.0, base=0,
                                             channel_multiplier=-1),
           reads=["identf"], writes=["identf"])
    ctx.op("dve", lambda v: v.tensor_copy(p.ident[:], p.ident_f[:]),
           reads=["identf"], writes=["ident"])


def emit_ln(p, pre, x_tiles, x_res, out_tiles, out_res, scr, gamma=None, beta=None,
            n=NE, width=Tc, stats_from=None, xbf_is_out=False):
    ctx = p.ctx
    nE = n
    if stats_from is None:
        xbf = out_tiles if xbf_is_out else scr["xbf"]
        alias = ([f"{out_res}{j}" for j in range(nE)] if xbf_is_out else [])
        for j in range(nE):
            ctx.op("dve", lambda v, j=j: v.tensor_copy(xbf[j][:, :width], x_tiles[j][:, :width]),
                   reads=[f"{x_res}{j}"],
                   writes=[f"{pre}xbf{j}"] + ([alias[j]] if alias else []))
        sf = xbf
        sf_res = [f"{pre}xbf{j}" for j in range(nE)]
    else:
        sf = stats_from
        sf_res = [f"{x_res}{j}" for j in range(nE)]
    psA, psB, psC = scr["psA"], scr["psB"], scr["psC"]
    xsq, mrow, vrow, rrow = scr["xsq"], scr["mrow"], scr["vrow"], scr["rrow"]
    mb, rb = scr["mb"], scr["rb"]
    psA_r, psB_r, psC_r = scr["psA_r"], scr["psB_r"], scr["psC_r"]

    for j in range(nE):
        ctx.op("te", lambda t, j=j: t.matmul(psA[0:1, :width], p.ones[:, 0:1],
                                             sf[j][:, :width],
                                             start=(j == 0), stop=(j == nE - 1)),
               reads=[sf_res[j], "ones"], writes=[psA_r])
    for j in range(nE):
        ctx.op("act", lambda s, j=j: s.activation(xsq[:, :width], sf[j][:, :width], AF.Square),
               reads=[sf_res[j]], writes=[f"{pre}xsq"])
        ctx.op("te", lambda t, j=j: t.matmul(psB[0:1, :width], p.ones[:, 0:1], xsq[:, :width],
                                             start=(j == 0), stop=(j == nE - 1)),
               reads=[f"{pre}xsq", "ones"], writes=[psB_r])
    ctx.op("act", lambda s: s.activation(mrow[:, :width], psA[0:1, :width], AF.Copy,
                                         scale=1.0 / (128 * nE)),
           reads=[psA_r], writes=[f"{pre}mrow"])
    ctx.op("dve", lambda v: v.scalar_tensor_tensor(vrow[:, :width], mrow[:, :width], -1.0,
                                                   mrow[:, :width], ALU.mult, ALU.mult),
           reads=[f"{pre}mrow"], writes=[f"{pre}vrow"])
    ctx.op("dve", lambda v: v.scalar_tensor_tensor(vrow[:, :width], psB[0:1, :width],
                                                   1.0 / (128 * nE), vrow[:, :width],
                                                   ALU.mult, ALU.add),
           reads=[psB_r, f"{pre}vrow"], writes=[f"{pre}vrow"])
    ctx.op("act", lambda s: s.activation(rrow[:, :width], vrow[:, :width], AF.Ln,
                                         bias=np.float32(LN_EPS)),
           reads=[f"{pre}vrow"], writes=[f"{pre}rrow"])
    ctx.op("act", lambda s: s.activation(rrow[:, :width], rrow[:, :width], AF.Exp,
                                         scale=-0.5),
           reads=[f"{pre}rrow"], writes=[f"{pre}rrow"])
    ctx.op("te", lambda t: t.matmul(psC[:, :width], p.ones_f[0:1, :], mrow[:, :width],
                                    start=True, stop=True),
           reads=[f"{pre}mrow", "ones"], writes=[psC_r])
    ctx.op("act", lambda s: s.copy(mb[:, :width], psC[:, :width]),
           reads=[psC_r], writes=[f"{pre}mb"])
    ctx.op("te", lambda t: t.matmul(psC[:, :width], p.ones_f[0:1, :], rrow[:, :width],
                                    start=True, stop=True),
           reads=[f"{pre}rrow", "ones"], writes=[psC_r])
    ctx.op("act", lambda s: s.copy(rb[:, :width], psC[:, :width]),
           reads=[psC_r], writes=[f"{pre}rb"])
    t1 = scr["t1"]
    for j in range(nE):
        ctx.op("dve", lambda v, j=j: v.tensor_sub(t1[:, :width], x_tiles[j][:, :width],
                                                  mb[:, :width]),
               reads=[f"{x_res}{j}", f"{pre}mb"], writes=[f"{pre}t1"])
        if gamma is None:
            ctx.op("dve", lambda v, j=j: v.tensor_mul(out_tiles[j][:, :width], t1[:, :width],
                                                      rb[:, :width]),
                   reads=[f"{pre}t1", f"{pre}rb"],
                   writes=[f"{out_res}{j}"] + ([f"{pre}xbf{j}"] if xbf_is_out else []))
        else:
            t2 = scr["t2"]
            ctx.op("dve", lambda v, j=j: v.tensor_mul(t2[:, :width], t1[:, :width],
                                                      rb[:, :width]),
                   reads=[f"{pre}t1", f"{pre}rb"], writes=[f"{pre}t2"])
            ctx.op("dve", lambda v, j=j: v.tensor_scalar(
                out_tiles[j][:, :width], t2[:, :width], gamma[:, j:j + 1], beta[:, j:j + 1],
                ALU.mult, ALU.add),
                reads=[f"{pre}t2", "gmbt"],
                writes=[f"{out_res}{j}"] + ([f"{pre}xbf{j}"] if xbf_is_out else []))


def emit_ln_residual(p, pre, f_tiles, f_res, x_tiles, x_res, scr, width=Tc):
    ctx = p.ctx
    emit_ln_stats_only(p, pre, f_tiles, f_res, scr, width)
    t1 = scr["t1"]
    mb, rb = scr["mb"], scr["rb"]
    for j in range(NE):
        ctx.op("dve", lambda v, j=j: v.tensor_sub(t1[:, :width], f_tiles[j][:, :width],
                                                  mb[:, :width]),
               reads=[f"{f_res}{j}", f"{pre}mb"], writes=[f"{pre}t1"])
        ctx.op("dve", lambda v, j=j: v.tensor_mul(t1[:, :width], t1[:, :width],
                                                  rb[:, :width]),
               reads=[f"{pre}t1", f"{pre}rb"], writes=[f"{pre}t1"])
        ctx.op("dve", lambda v, j=j: v.tensor_add(x_tiles[j][:, :width], x_tiles[j][:, :width],
                                                  t1[:, :width]),
               reads=[f"{pre}t1", f"{x_res}{j}"], writes=[f"{x_res}{j}"])


def emit_ln_stats_only(p, pre, sf, sf_res_pref, scr, width=Tc, n=NE):
    ctx = p.ctx
    psA, psB, psC = scr["psA"], scr["psB"], scr["psC"]
    psA_r, psB_r, psC_r = scr["psA_r"], scr["psB_r"], scr["psC_r"]
    xsq, mrow, vrow, rrow = scr["xsq"], scr["mrow"], scr["vrow"], scr["rrow"]
    mb, rb = scr["mb"], scr["rb"]
    for j in range(n):
        ctx.op("te", lambda t, j=j: t.matmul(psA[0:1, :width], p.ones[:, 0:1],
                                             sf[j][:, :width],
                                             start=(j == 0), stop=(j == n - 1)),
               reads=[f"{sf_res_pref}{j}"], writes=[psA_r])
    for j in range(n):
        ctx.op("act", lambda s, j=j: s.activation(xsq[:, :width], sf[j][:, :width], AF.Square),
               reads=[f"{sf_res_pref}{j}"], writes=[f"{pre}xsq"])
        ctx.op("te", lambda t, j=j: t.matmul(psB[0:1, :width], p.ones[:, 0:1], xsq[:, :width],
                                             start=(j == 0), stop=(j == n - 1)),
               reads=[f"{pre}xsq"], writes=[psB_r])
    ctx.op("act", lambda s: s.activation(mrow[:, :width], psA[0:1, :width], AF.Copy,
                                         scale=1.0 / (128 * n)),
           reads=[psA_r], writes=[f"{pre}mrow"])
    ctx.op("dve", lambda v: v.scalar_tensor_tensor(vrow[:, :width], mrow[:, :width], -1.0,
                                                   mrow[:, :width], ALU.mult, ALU.mult),
           reads=[f"{pre}mrow"], writes=[f"{pre}vrow"])
    ctx.op("dve", lambda v: v.scalar_tensor_tensor(vrow[:, :width], psB[0:1, :width],
                                                   1.0 / (128 * n), vrow[:, :width],
                                                   ALU.mult, ALU.add),
           reads=[psB_r, f"{pre}vrow"], writes=[f"{pre}vrow"])
    ctx.op("act", lambda s: s.activation(rrow[:, :width], vrow[:, :width], AF.Ln,
                                         bias=np.float32(LN_EPS)),
           reads=[f"{pre}vrow"], writes=[f"{pre}rrow"])
    ctx.op("act", lambda s: s.activation(rrow[:, :width], rrow[:, :width], AF.Exp,
                                         scale=-0.5),
           reads=[f"{pre}rrow"], writes=[f"{pre}rrow"])
    ctx.op("te", lambda t: t.matmul(psC[:, :width], p.ones_f[0:1, :], mrow[:, :width],
                                    start=True, stop=True),
           reads=[f"{pre}mrow", "ones"], writes=[psC_r])
    ctx.op("act", lambda s: s.copy(mb[:, :width], psC[:, :width]),
           reads=[psC_r], writes=[f"{pre}mb"])
    ctx.op("te", lambda t: t.matmul(psC[:, :width], p.ones_f[0:1, :], rrow[:, :width],
                                    start=True, stop=True),
           reads=[f"{pre}rrow", "ones"], writes=[psC_r])
    ctx.op("act", lambda s: s.copy(rb[:, :width], psC[:, :width]),
           reads=[psC_r], writes=[f"{pre}rb"])


def make_ln_scratch(p, tag="", width=None, no_xbf=False):
    w = width or Tc
    scr = {
        "xbf": (None if no_xbf else
                [p.sb(f"lnxbf{tag}{j}", [128, w], BF16) for j in range(NE)]),
        "xsq": p.sb(f"lnxsq{tag}", [128, w], BF16),
        "mrow": p.sb(f"lnm{tag}", [1, w], F32),
        "vrow": p.sb(f"lnv{tag}", [1, w], F32),
        "rrow": p.sb(f"lnr{tag}", [1, w], F32),
        "mb": p.sb(f"lnmb{tag}", [128, w], F32),
        "rb": p.sb(f"lnrb{tag}", [128, w], F32),
        "t1": p.sb(f"lnt1{tag}", [128, w], F32),
        "t2": p.sb(f"lnt2{tag}", [128, w], F32),
    }
    return scr



def build_stage_l0attn(debug=99.0, p=None, io=None):
    standalone = p is None
    if standalone:
        p = Prog("l0attn").__enter__()
    nc, ctx = p.nc, p.ctx
    xT = io["xT"] if io else p.din("xT", [E, S], F32)
    w_qkT = p.din("w_qkT", [E, 2048], F32)
    w_vT = p.din("w_vT", [E, 1024], F32)
    b_inP = p.din("b_inP", [128, 24], F32)
    w_oT = p.din("w_oT", [E, E], F32)
    b_oP = p.din("b_oP", [128, 8], F32)
    yT = io["yT"] if io else p.dout("yT", [E, S], F32)

    setup_common(p)
    PS = p.psum
    psT = p._psum_bf16

    wqk = p.sb("wqk", [128, NE * 2048], BF16)
    wv = p.sb("wv", [128, NE * 1024], BF16)
    wo = p.sb("wo", [128, NE * 1024], BF16)
    bin_sb = p.sb("bin_sb", [128, 24], F32)
    bo_sb = p.sb("bo_sb", [128, 8], F32)
    for j in range(NE):
        ctx.dma("pool", wqk[:, j * 2048:(j + 1) * 2048], w_qkT[j * 128:(j + 1) * 128, :],
                writes=[f"wqk{j}"])
        ctx.dma("pool", wv[:, j * 1024:(j + 1) * 1024], w_vT[j * 128:(j + 1) * 128, :],
                writes=[f"wv{j}"])
        ctx.dma("pool", wo[:, j * 1024:(j + 1) * 1024], w_oT[j * 128:(j + 1) * 128, :],
                writes=[f"wo{j}"])
    ctx.dma("qs", bin_sb[:], b_inP, writes=["bin"])
    ctx.dma("qs", bo_sb[:], b_oP, writes=["bo"])

    mask2 = p.sb("mask2", [128, 512], F32)
    ctx.op("pool", lambda g: g.memset(mask2[:], 0.0), writes=["mask2"])
    for hh in range(2):
        ctx.op("pool", lambda g, hh=hh: g.affine_select(
            mask2[:, hh * 256:(hh + 1) * 256], mask2[:, hh * 256:(hh + 1) * 256],
            [[1, 256]], ALU.is_ge, -1e9, base=0, channel_multiplier=-1),
            reads=["mask2"], writes=["mask2"])
        ctx.op("pool", lambda g, hh=hh: g.affine_select(
            mask2[:, hh * 256:(hh + 1) * 256], mask2[:, hh * 256:(hh + 1) * 256],
            [[-1, 256]], ALU.is_ge, -1e9, base=WIN, channel_multiplier=1),
            reads=["mask2"], writes=["mask2"])
    mask2f = p.sb("mask2f", [128, 512], F32)
    ctx.op("pool", lambda g: g.tensor_copy(mask2f[:], mask2[:]),
           reads=["mask2"], writes=["mask2f"])
    for hh in range(2):
        ctx.op("pool", lambda g, hh=hh: g.affine_select(
            mask2f[:, hh * 256:(hh + 1) * 256], mask2f[:, hh * 256:(hh + 1) * 256],
            [[1, 256]], ALU.is_ge, -1e9, base=-128, channel_multiplier=0),
            reads=["mask2f"], writes=["mask2f"])

    xt = [p.sb(f"xt{j}", [128, Tc], F32) for j in range(NE)]
    lnx = [p.sb(f"lnxx{j}", [128, Tc], BF16) for j in range(NE)]
    scr = make_ln_scratch(p)
    scr["psA"], scr["psB"], scr["psC"] = PS[4], PS[5], PS[6]
    scr["psA_r"], scr["psB_r"], scr["psC_r"] = "ps4", "ps5", "ps6"

    qh = p.sb("qh", [64, H * Tc], BF16)
    KW = WIN + Tc
    kh = p.sb("kh", [64, H * KW], BF16)
    NV = (WIN + Tc) // 128
    v_sb = p.sb("v_sb", [128, NV * 1024], BF16)
    NPAR = 4
    sm = [p.sb(f"sm{i}", [128, 512], F32) for i in range(NPAR)]
    nmax = [p.sb(f"nmax{i}", [128, 1], F32) for i in range(NPAR)]
    den = [p.sb(f"den{i}", [128, 2], F32) for i in range(NPAR)]
    rden = [p.sb(f"rden{i}", [128, 2], F32) for i in range(NPAR)]
    p_scr = [p.sb(f"p_scr{i}", [128, 512], BF16) for i in range(NPAR)]
    p_sb = [p.sb(f"p_sb{i}", [128, 512], BF16) for i in range(NPAR)]
    pt_sb = [p.sb(f"pt_sb{i}", [128, 512], BF16) for i in range(NPAR)]
    fbf = [p.sb(f"fbf{j}", [128, Tc], BF16) for j in range(NE)]
    obf = [p.sb(f"obf{j}", [128, Tc], BF16) for j in range(NE)]

    ctx.op("pool", lambda g: g.memset(kh[:], 0.0), writes=["kh"])
    ctx.op("pool", lambda g: g.memset(v_sb[:, 0:1024], 0.0), writes=["vch"])

    for c in range(NCH):
        t0 = c * Tc
        for j in range(NE):
            ctx.dma("pool", xt[j][:], xT[j * 128:(j + 1) * 128, t0:t0 + Tc],
                    writes=[f"x{j}"])
        emit_ln(p, f"L{c}_", xt, "x", lnx, "lnx", scr)

        if c > 0:
            ctx.op("dve", lambda v: v.tensor_copy(
                kh[:].rearrange("p (h w) -> p h w", h=H)[:, :, 0:WIN],
                kh[:].rearrange("p (h w) -> p h w", h=H)[:, :, Tc:Tc + WIN]),
                reads=["kh"], writes=["kh"])
            ctx.op("dve", lambda v: v.tensor_copy(v_sb[:, 0:1024],
                                                  v_sb[:, (NV - 1) * 1024:NV * 1024]),
                   reads=["vch"], writes=["vch"])

        for fo in (range(NE) if debug >= 1 else []):
            pb = PS[fo % 2]
            pr = f"ps{fo % 2}"
            for j in range(NE):
                ctx.op("te", lambda t, fo=fo, j=j, pb=pb: t.matmul(
                    pb[:], wqk[:, j * 2048 + fo * 128: j * 2048 + (fo + 1) * 128],
                    lnx[j][:, :], start=(j == 0), stop=(j == NE - 1)),
                    reads=[f"lnx{j}", f"wqk{j}"], writes=[pr])
            for hh in range(2):
                ctx.op("act", lambda s, fo=fo, pb=pb, hh=hh: s.activation(
                    qh[:, (2 * fo + hh) * Tc:(2 * fo + hh + 1) * Tc],
                    pb[hh * 64:(hh + 1) * 64, :], AF.Identity,
                    bias=bin_sb[hh * 64:(hh + 1) * 64, fo:fo + 1]),
                    reads=[pr, "bin"], writes=["qhh"])
        for fo in (range(NE) if debug >= 1 else []):
            pb = PS[fo % 2]
            pr = f"ps{fo % 2}"
            for j in range(NE):
                ctx.op("te", lambda t, fo=fo, j=j, pb=pb: t.matmul(
                    pb[:], wqk[:, j * 2048 + 1024 + fo * 128: j * 2048 + 1024 + (fo + 1) * 128],
                    lnx[j][:, :], start=(j == 0), stop=(j == NE - 1)),
                    reads=[f"lnx{j}", f"wqk{j}"], writes=[pr])
            for hh in range(2):
                ctx.op("act", lambda s, fo=fo, pb=pb, hh=hh: s.activation(
                    kh[:, (2 * fo + hh) * KW + WIN:(2 * fo + hh) * KW + WIN + Tc],
                    pb[hh * 64:(hh + 1) * 64, :], AF.Identity,
                    bias=bin_sb[hh * 64:(hh + 1) * 64, 8 + fo:9 + fo]),
                    reads=[pr, "bin"], writes=["kh"])
        if debug < 1:
            for j in range(NE):
                ctx.op("dve", lambda v, j=j: v.tensor_copy(q_sb[j][:], lnx[j][:]),
                       reads=[f"lnx{j}"], writes=[f"qch{j}"])
        for vt in (range(QT) if debug >= 2 else []):
            for half in range(2):
                pb = PS[2 + half]
                pr = f"ps{2 + half}"
                for j in range(NE):
                    ctx.op("te", lambda t, vt=vt, half=half, j=j, pb=pb: t.matmul(
                        pb[:], lnx[j][:, vt * 128:(vt + 1) * 128],
                        wv[:, j * 1024 + half * 512: j * 1024 + (half + 1) * 512],
                        start=(j == 0), stop=(j == NE - 1)),
                        reads=[f"lnx{j}", f"wv{j}"], writes=[pr])
                ctx.op("act", lambda s, vt=vt, half=half, pb=pb: s.copy(
                    v_sb[:, (vt + 1) * 1024 + half * 512:(vt + 1) * 1024 + (half + 1) * 512],
                    pb[:]),
                    reads=[pr], writes=["vch"])

        if debug < 2.25:
            for j in range(NE):
                ctx.op("dve", lambda v, j=j: v.tensor_copy(fbf[j][:], lnx[j][:]),
                       reads=[f"lnx{j}"], writes=[f"fbf{j}"])
        for qt in (range(QT) if debug >= 2.25 else []):
            msk = mask2f if (c == 0 and qt == 0) else mask2
            msk_r = "mask2f" if (c == 0 and qt == 0) else "mask2"
            for ht in range(NE):
                sps = PS[2 + (ht % 2)]
                sps_r = f"ps{2 + (ht % 2)}"
                for hh in range(2):
                    head = ht * 2 + hh
                    ctx.op("te", lambda t, qt=qt, head=head, hh=hh, sps=sps: t.matmul(
                        sps[:, hh * 256:(hh + 1) * 256],
                        qh[:, head * Tc + qt * 128: head * Tc + (qt + 1) * 128],
                        kh[:, head * KW + qt * 128: head * KW + qt * 128 + 256],
                        start=True, stop=True),
                        reads=["qhh", "kh"], writes=[sps_r])
                bp = (qt * NE + ht) % NPAR
                smb, nmb, denb, rdenb = sm[bp], nmax[bp], den[bp], rden[bp]
                pscrb, psbb, ptb = p_scr[bp], p_sb[bp], pt_sb[bp]
                ctx.op("dve", lambda v, sps=sps, nmb=nmb: v.reduce_max(
                    nmb[:], sps[:], axis=AX.X, negate=True),
                    reads=[sps_r], writes=[f"nmax{bp}"], late=[f"nmax{bp}"])
                ctx.op("dve", lambda v, sps=sps, msk=msk, smb=smb, nmb=nmb:
                       v.scalar_tensor_tensor(
                           smb[:], sps[:], nmb[:, 0:1], msk[:], ALU.add, ALU.add),
                       reads=[sps_r, msk_r], sreads=[f"nmax{bp}"], writes=[f"sm{bp}"])
                for hh in range(2):
                    ctx.op("act", lambda s, hh=hh, smb=smb, pscrb=pscrb, denb=denb:
                           s.activation(
                               pscrb[:, hh * 256:(hh + 1) * 256],
                               smb[:, hh * 256:(hh + 1) * 256], AF.Exp, scale=QFS,
                               accum_out=denb[:, hh:hh + 1]),
                           reads=[f"sm{bp}"], writes=[f"p_scr{bp}", f"den{bp}"],
                           late=[f"den{bp}"])
                ctx.op("dve", lambda v, denb=denb, rdenb=rdenb: v.reciprocal(
                    rdenb[:], denb[:]),
                    reads=[f"den{bp}"], writes=[f"rden{bp}"], late=[f"rden{bp}"])
                for hh in range(2):
                    ctx.op("dve", lambda v, hh=hh, pscrb=pscrb, psbb=psbb, rdenb=rdenb:
                           v.tensor_scalar_mul(
                               psbb[:, hh * 256:(hh + 1) * 256],
                               pscrb[:, hh * 256:(hh + 1) * 256], rdenb[:, hh:hh + 1]),
                           reads=[f"p_scr{bp}"], sreads=[f"rden{bp}"],
                           writes=[f"p_sb{bp}"])
                for hh in (range(2) if debug >= 2.5 else []):
                    for kt in range(2):
                        ctx.op("te", lambda t, hh=hh, kt=kt, psbb=psbb: t.transpose(
                            psT[:, (hh * 2 + kt) * 128:(hh * 2 + kt + 1) * 128],
                            psbb[:, hh * 256 + kt * 128: hh * 256 + (kt + 1) * 128],
                            p.ident[:]),
                            reads=[f"p_sb{bp}", "ident"], writes=["ps7"])
                if debug >= 2.5:
                    ctx.op("act", lambda s, ptb=ptb: s.copy(ptb[:], psT[:, 0:512]),
                           reads=["ps7"], writes=[f"pt{bp}"])
                ob = PS[4 + (ht % 2)]
                ob_r = f"ps{4 + (ht % 2)}"
                for hh in (range(2) if debug >= 2.75 else []):
                    head = ht * 2 + hh
                    for kt in range(2):
                        vtile = qt + kt
                        ctx.op("te", lambda t, hh=hh, kt=kt, vtile=vtile, head=head, ob=ob, qt=qt, ptb=ptb: t.matmul(
                            ob[hh * 64:(hh + 1) * 64, qt * 128:(qt + 1) * 128],
                            v_sb[:, vtile * 1024 + head * 64: vtile * 1024 + (head + 1) * 64],
                            ptb[:, (hh * 2 + kt) * 128:(hh * 2 + kt + 1) * 128],
                            start=(kt == 0), stop=(kt == 1)),
                            reads=[f"pt{bp}", "vch"], writes=[ob_r])
                if debug >= 2.75:
                    ctx.op("act", lambda s, ht=ht, qt=qt, ob=ob: s.activation(
                        fbf[ht][:, qt * 128:(qt + 1) * 128], ob[:, qt * 128:(qt + 1) * 128],
                        AF.Identity, bias=bin_sb[:, 16 + ht:17 + ht]),
                        reads=[ob_r, "bin"], writes=[f"fbf{ht}"])
                else:
                    ctx.op("dve", lambda v, ht=ht, qt=qt: v.tensor_copy(
                        fbf[ht][:, qt * 128:(qt + 1) * 128],
                        p_sb[:, 0:128]),
                        reads=["p_sb"], writes=[f"fbf{ht}"])

        for fo in (range(NE) if debug >= 3 else []):
            pb = PS[fo % 2]
            pr = f"ps{fo % 2}"
            for j in range(NE):
                ctx.op("te", lambda t, fo=fo, j=j, pb=pb: t.matmul(
                    pb[:], wo[:, j * 1024 + fo * 128: j * 1024 + (fo + 1) * 128],
                    fbf[j][:, :], start=(j == 0), stop=(j == NE - 1)),
                    reads=[f"fbf{j}", f"wo{j}"], writes=[pr])
            ctx.op("act", lambda s, fo=fo, pb=pb: s.activation(
                obf[fo][:], pb[:], AF.Identity, bias=bo_sb[:, fo:fo + 1]),
                reads=[pr, "bo"], writes=[f"obf{fo}"])

        if debug < 3:
            for j in range(NE):
                ctx.op("dve", lambda v, j=j: v.tensor_copy(obf[j][:], fbf[j][:]),
                       reads=[f"fbf{j}"], writes=[f"obf{j}"])
        emit_ln_residual(p, f"R{c}_", obf, "obf", xt, "x", scr)
        for j in range(NE):
            ctx.dma("qs", yT[j * 128:(j + 1) * 128, t0:t0 + Tc], xt[j][:],
                    reads=[f"x{j}"], writes=[f"y{j}"])
        ctx.barrier()

    return p.close() if standalone else None



def build_stage_l0ffn(p=None, io=None):
    standalone = p is None
    if standalone:
        p = Prog("l0ffn").__enter__()
    nc, ctx = p.nc, p.ctx
    xT = io["xT"] if io else p.din("xT", [E, S], F32)
    wgT = p.din("wgT", [E, FH], I32)
    wuT = p.din("wuT", [E, FH], I32)
    wdT = p.din("wdT", [FH, E], I32)
    yT = io["yT"] if io else p.dout("yT", [E, S], F32)

    setup_common(p)
    PS = p.psum

    wg = p.sb("wg", [128, NE * FH], BF16)
    wu = p.sb("wu", [128, NE * FH], BF16)
    for j in range(NE):
        ctx.dma("pool", wg[:, j * FH:(j + 1) * FH], wgT[j * 128:(j + 1) * 128, :],
                writes=[f"wg{j}"])
        ctx.dma("pool", wu[:, j * FH:(j + 1) * FH], wuT[j * 128:(j + 1) * 128, :],
                writes=[f"wu{j}"])
    wdb = [p.sb(f"wdb{i}", [128, NF * 128], BF16) for i in range(3)]
    wdT_r = wdT.rearrange("(j p) e -> j p e", p=128)
    wd_bf = nc.dram_tensor("wd_bf", [NE, 128, NF * 128], BF16, kind="Internal").ap()
    for fo in range(NE):
        wb = wdb[fo % 3]
        wb_r = f"wdb{fo % 3}"
        ctx.dma("pool", wb[:].rearrange("p (j e) -> p j e", e=128),
                wdT_r[:, :, fo * 128:(fo + 1) * 128].rearrange("j p e -> p j e"),
                writes=[wb_r])
        ctx.dma("qs", wd_bf[fo], wb[:], reads=[wb_r], writes=[f"wdbf{fo}"])

    xt = [p.sb(f"xt{j}", [128, Tc], F32) for j in range(NE)]
    lnx = [p.sb(f"lnxx{j}", [128, Tc], BF16) for j in range(NE)]
    scr = make_ln_scratch(p)
    scr["psA"], scr["psB"], scr["psC"] = PS[4], PS[5], PS[6]
    scr["psA_r"], scr["psB_r"], scr["psC_r"] = "ps4", "ps5", "ps6"
    hb = p.sb("hb", [128, NF * Tc], BF16)
    gs = p.sb("gs", [128, Tc], BF16)
    us = p.sb("us", [128, Tc], BF16)
    obf = [p.sb(f"obf{j}", [128, Tc], BF16) for j in range(NE)]

    SC1 = np.float32(1.0 / 127.0)
    SC2 = np.float32(1.0 / (127.0 * 127.0))

    for c in range(NCH):
        t0 = c * Tc
        for j in range(NE):
            ctx.dma("pool", xt[j][:], xT[j * 128:(j + 1) * 128, t0:t0 + Tc],
                    writes=[f"x{j}"])
        emit_ln(p, f"L{c}_", xt, "x", lnx, "lnx", scr)

        for fo in range(NF):
            pg = PS[fo % 2]
            pg_r = f"ps{fo % 2}"
            pu = PS[2 + fo % 2]
            pu_r = f"ps{2 + fo % 2}"
            for j in range(NE):
                ctx.op("te", lambda t, fo=fo, j=j, pg=pg: t.matmul(
                    pg[:], wg[:, j * FH + fo * 128: j * FH + (fo + 1) * 128],
                    lnx[j][:], start=(j == 0), stop=(j == NE - 1)),
                    reads=[f"lnx{j}", f"wg{j}"], writes=[pg_r])
            for j in range(NE):
                ctx.op("te", lambda t, fo=fo, j=j, pu=pu: t.matmul(
                    pu[:], wu[:, j * FH + fo * 128: j * FH + (fo + 1) * 128],
                    lnx[j][:], start=(j == 0), stop=(j == NE - 1)),
                    reads=[f"lnx{j}", f"wu{j}"], writes=[pu_r])
            ctx.op("act", lambda s, pg=pg: s.activation(gs[:], pg[:], AF.Silu, scale=SC1),
                   reads=[pg_r], writes=["gs"])
            ctx.op("act", lambda s, pu=pu: s.copy(us[:], pu[:]),
                   reads=[pu_r], writes=["us"])
            ctx.op("dve", lambda v, fo=fo: v.tensor_mul(
                hb[:, fo * Tc:(fo + 1) * Tc], gs[:], us[:]),
                reads=["gs", "us"], writes=[f"hb{fo}"])

        for fo in range(NE):
            wb = wdb[fo % 3]
            wb_r = f"wdb{fo % 3}"
            ctx.dma("qs", wb[:], wd_bf[fo], reads=[f"wdbf{fo}"], writes=[wb_r])
            pd = PS[fo % 2]
            pd_r = f"ps{fo % 2}"
            for j in range(NF):
                ctx.op("te", lambda t, fo=fo, j=j, pd=pd, wb=wb: t.matmul(
                    pd[:], wb[:, j * 128:(j + 1) * 128],
                    hb[:, j * Tc:(j + 1) * Tc], start=(j == 0), stop=(j == NF - 1)),
                    reads=[f"hb{j}", wb_r], writes=[pd_r])
            ctx.op("act", lambda s, fo=fo, pd=pd: s.activation(
                obf[fo][:], pd[:], AF.Identity, scale=SC2),
                reads=[pd_r], writes=[f"obf{fo}"])

        emit_ln_residual(p, f"R{c}_", obf, "obf", xt, "x", scr)
        for j in range(NE):
            ctx.dma("qs", yT[j * 128:(j + 1) * 128, t0:t0 + Tc], xt[j][:],
                    reads=[f"x{j}"], writes=[f"y{j}"])
        ctx.barrier()
    return p.close() if standalone else None



def build_stage_l1mamba(dbg=False, p=None, io=None):
    standalone = p is None
    if standalone:
        p = Prog("l1mamba").__enter__()
    nc, ctx = p.nc, p.ctx
    xT = io["xT"] if io else p.din("xT", [E, S], F32)
    W_in = p.din("W_in", [E, 2 * DIN], F32)
    conv_wP = p.din("conv_wP", [128, ND * DCONV], F32)
    conv_bP = p.din("conv_bP", [128, ND], F32)
    W_x = p.din("W_x", [DIN, DTR + 2 * DST], F32)
    W_dt = p.din("W_dt", [DTR, DIN], F32)
    b_dtP = p.din("b_dtP", [128, ND], F32)
    A_logP = p.din("A_logP", [128, ND * DST], F32)
    D_P = p.din("D_P", [128, ND], F32)
    W_out = p.din("W_out", [DIN, E], F32)
    gammaP = p.din("gammaP", [128, NE], F32)
    betaP = p.din("betaP", [128, NE], F32)
    yT = io["yT"] if io else p.dout("yT", [E, S], F32)

    setup_common(p)
    PS = p.psum

    cw = p.sb("cw", [128, ND * DCONV], F32)
    cb = p.sb("cb", [128, ND], F32)
    bdt = p.sb("bdt", [128, ND], F32)
    alog = p.sb("alog", [128, ND * DST], F32)
    ascan = p.sb("ascan", [128, ND * DST], F32)
    dP = p.sb("dP", [128, ND], F32)
    gmb = p.sb("gmb", [128, NE], F32)
    btb = p.sb("btb", [128, NE], F32)
    wx = p.sb("wx", [128, ND * 96], BF16)
    wdt = p.sb("wdt", [64, DIN], BF16)
    sel = p.sb("sel", [16, 16 * 128], BF16)
    p.push_scope()
    sel_f = p.sb("sel_f", [16, 16 * 128], F32)
    ctx.dma("qs", cw[:], conv_wP, writes=["cw"])
    ctx.dma("qs", cb[:], conv_bP, writes=["cb"])
    ctx.dma("qs", bdt[:], b_dtP, writes=["bdt"])
    ctx.dma("qs", alog[:], A_logP, writes=["alog"])
    ctx.dma("qs", dP[:], D_P, writes=["dP"])
    ctx.dma("qs", gmb[:], gammaP, writes=["gmbt"])
    ctx.dma("qs", btb[:], betaP, writes=["gmbt"])
    ctx.dma("pool", wdt[:], W_dt, writes=["wdt"])
    for g in range(ND):
        ctx.dma("pool", wx[:, g * 96:(g + 1) * 96], W_x[g * 128:(g + 1) * 128, :],
                writes=["wx"])
    ctx.op("act", lambda s: s.activation(ascan[:], alog[:], AF.Exp),
           reads=["alog"], writes=["ascan"])
    ctx.op("dve", lambda v: v.tensor_scalar_mul(ascan[:], ascan[:], -1.0),
           reads=["ascan"], writes=["ascan"])
    ctx.op("pool", lambda g: g.memset(sel_f[:], 1.0), writes=["self"])
    ctx.op("pool", lambda g: g.affine_select(
        sel_f[:].rearrange("p (j c) -> p j c", c=128), sel_f[:].rearrange("p (j c) -> p j c", c=128),
        [[1, 16], [0, 128]], ALU.is_equal, 0.0, base=0, channel_multiplier=-1),
        reads=["self"], writes=["self"])
    ctx.op("dve", lambda v: v.tensor_copy(sel[:], sel_f[:]), reads=["self"], writes=["sel"])
    p.pop_scope()
    ctx.barrier()

    winb = [p.sb(f"winb{i}", [128, NE * 128], BF16) for i in range(2)]
    W_in_r = W_in.rearrange("(j p) d -> j p d", p=128)
    woutb = [p.sb(f"woutb{i}", [128, ND * 128], BF16) for i in range(2)]
    W_out_r = W_out.rearrange("(j p) e -> j p e", p=128)
    win_bf = nc.dram_tensor("win_bf", [2 * ND, 128, NE * 128], BF16,
                            kind="Internal").ap()
    wout_bf = nc.dram_tensor("wout_bf", [NE, 128, ND * 128], BF16,
                             kind="Internal").ap()
    for g in range(2 * ND):
        wb = winb[g % 2]
        wb_r = f"winb{g % 2}"
        ctx.dma("pool", wb[:].rearrange("p (j d) -> p j d", d=128),
                W_in_r[:, :, g * 128:(g + 1) * 128].rearrange("j p d -> p j d"),
                writes=[wb_r])
        ctx.dma("qs", win_bf[g], wb[:], reads=[wb_r], writes=[f"winbf{g}"])
    for fo in range(NE):
        wb = woutb[fo % 2]
        wb_r = f"woutb{fo % 2}"
        ctx.dma("pool", wb[:].rearrange("p (j e) -> p j e", e=128),
                W_out_r[:, :, fo * 128:(fo + 1) * 128].rearrange("j p e -> p j e"),
                writes=[wb_r])
        ctx.dma("qs", wout_bf[fo], wb[:], reads=[wb_r], writes=[f"woutbf{fo}"])

    xt = [p.sb(f"xt{j}", [128, Tc], F32) for j in range(NE)]
    lnx = [p.sb(f"lnxx{j}", [128, Tc], BF16) for j in range(NE)]
    scr = make_ln_scratch(p, no_xbf=True)
    scr["psA"], scr["psB"], scr["psC"] = PS[4], PS[5], PS[6]
    scr["psA_r"], scr["psB_r"], scr["psC_r"] = "ps4", "ps5", "ps6"
    xin = p.sb("xin", [128, ND * (Tc + 3)], BF16)
    xc = p.sb("xc", [128, ND * Tc], BF16)
    zrec = [p.sb("zrec0", [128, Tc], BF16)] * 2
    dtb = p.sb("dtb", [64, Tc], BF16)
    bcf = p.sb("bcf", [16, 2 * Tc], BF16)
    delta = p.sb("delta", [128, ND * Tc], BF16)
    du = p.sb("du", [128, ND * Tc], BF16)
    conv_acc = p.sb("conv_acc", [128, Tc], F32)
    conv_accb = p.sb("conv_accb", [128, Tc], BF16)
    e_scr = conv_acc
    state = p.sb("state", [128, ND * DST], F32)
    a_t = [p.sb("a_t0", [128, Tc], F32), p.sb("a_t1", [128, Tc], F32)]
    b_stk = p.sb("b_stk", [128, DST * Tc], BF16)
    h_st = p.sb("h_st", [128, DST * Tc], BF16)
    hc_st = h_st
    B_st = p.sb("B_st", [128, DST * Tc], BF16)
    C_st = p.sb("C_st", [128, DST * Tc], BF16)
    yz = du
    mo = [lnx[j] for j in range(NE)]
    h1 = [delta[:, j * Tc:(j + 1) * Tc] for j in range(NE)]
    gout = [xc[:, j * Tc:(j + 1) * Tc] for j in range(NE)]

    st_all = [f"st{g}_{si}" for g in range(ND) for si in range(DST)]
    dbf = conv_acc
    ctx.op("pool", lambda g: g.memset(state[:], 0.0), writes=st_all)
    ctx.op("pool", lambda g: g.memset(xin[:], 0.0), writes=["xin_halo"])

    for c in range(NCH):
        t0 = c * Tc
        for j in range(NE):
            ctx.dma("pool", xt[j][:], xT[j * 128:(j + 1) * 128, t0:t0 + Tc],
                    writes=[f"x{j}"])
        emit_ln(p, f"L{c}_", xt, "x", lnx, "lnx", scr, xbf_is_out=True)

        if c > 0:
            ctx.op("dve", lambda v: v.tensor_copy(
                xin[:].rearrange("p (g w) -> p g w", g=ND)[:, :, 0:3],
                xin[:].rearrange("p (g w) -> p g w", g=ND)[:, :, Tc:Tc + 3]),
                reads=[f"xin{gg}" for gg in range(ND)], writes=["xin_halo"])

        for g in range(ND):
            wb = winb[g % 2]
            wb_r = f"winb{g % 2}"
            ctx.dma("qs", wb[:], win_bf[g], reads=[f"winbf{g}"], writes=[wb_r])
            pb = PS[g % 2]
            pr = f"ps{g % 2}"
            for j in range(NE):
                ctx.op("te", lambda t, g=g, j=j, pb=pb, wb=wb: t.matmul(
                    pb[:], wb[:, j * 128:(j + 1) * 128], lnx[j][:],
                    start=(j == 0), stop=(j == NE - 1)),
                    reads=[f"lnx{j}", wb_r], writes=[pr])
            ctx.op("act", lambda s, g=g, pb=pb: s.copy(
                xin[:, g * (Tc + 3) + 3: (g + 1) * (Tc + 3)], pb[:]),
                reads=[pr], writes=[f"xin{g}"], tag=f"xinev{g}")

        for g in range(ND):
            xg = xin[:, g * (Tc + 3): (g + 1) * (Tc + 3)]
            ctx.op("dve", lambda v, g=g, xg=xg: v.tensor_scalar(
                conv_accb[:], xg[:, 0:Tc], cw[:, g * DCONV: g * DCONV + 1],
                cb[:, g:g + 1], ALU.mult, ALU.add),
                reads=[f"xin{g}", "xin_halo", "cw", "cb"], writes=["conv_accb"])
            for kk in range(1, DCONV):
                ctx.op("dve", lambda v, g=g, kk=kk, xg=xg: v.scalar_tensor_tensor(
                    conv_accb[:], xg[:, kk:kk + Tc], cw[:, g * DCONV + kk: g * DCONV + kk + 1],
                    conv_accb[:], ALU.mult, ALU.add),
                    reads=[f"xin{g}", "xin_halo", "cw", "conv_accb"], writes=["conv_accb"])
            ctx.op("act", lambda s, g=g: s.activation(
                xc[:, g * Tc:(g + 1) * Tc], conv_accb[:], AF.Silu),
                reads=["conv_accb"], writes=["xc"])

        for (lo, hi, pbidx, nm) in ((0, 64, 2, "dt"), (64, 80, 3, "B"), (80, 96, 3, "C")):
            pb = PS[pbidx]
            pr = f"ps{pbidx}"
            if nm == "C":
                pb = PS[2]
                pr = "ps2"
            for g in range(ND):
                ctx.op("te", lambda t, g=g, lo=lo, hi=hi, pb=pb: t.matmul(
                    pb[0:hi - lo, :], wx[:, g * 96 + lo: g * 96 + hi],
                    xc[:, g * Tc:(g + 1) * Tc],
                    start=(g == 0), stop=(g == ND - 1)),
                    reads=["xc", "wx"], writes=[pr])
            if nm == "dt":
                ctx.op("act", lambda s, pb=pb: s.copy(dtb[:], pb[0:64, :]),
                       reads=[pr], writes=["dtb"])
            elif nm == "B":
                ctx.op("act", lambda s, pb=pb: s.copy(bcf[:, 0:Tc], pb[0:16, :]),
                       reads=[pr], writes=["bcf"])
            else:
                ctx.op("act", lambda s, pb=pb: s.copy(bcf[:, Tc:2 * Tc], pb[0:16, :]),
                       reads=[pr], writes=["bcf"])

        for g in range(ND):
            pb = PS[2 + g % 2]
            pr = f"ps{2 + g % 2}"
            ctx.op("te", lambda t, g=g, pb=pb: t.matmul(
                pb[:], wdt[:, g * 128:(g + 1) * 128], dtb[:],
                start=True, stop=True),
                reads=["dtb", "wdt"], writes=[pr])
            ctx.op("act", lambda s, g=g, pb=pb: s.activation(
                e_scr[:], pb[:], AF.Exp, bias=bdt[:, g:g + 1]),
                reads=[pr, "bdt"], writes=["conv_acc"])
            ctx.op("act", lambda s, g=g: s.activation(
                delta[:, g * Tc:(g + 1) * Tc], e_scr[:], AF.Ln, bias=np.float32(1.0)),
                reads=["conv_acc"], writes=[f"dl{g}"])
        for g in range(ND):
            ctx.op("dve", lambda v, g=g: v.tensor_mul(
                du[:, g * Tc:(g + 1) * Tc], delta[:, g * Tc:(g + 1) * Tc],
                xc[:, g * Tc:(g + 1) * Tc]),
                reads=[f"dl{g}", "xc", "yz"], writes=[f"du{g}"])

        for si in range(DST):
            pB = PS[2]
            pC = PS[3]
            ctx.op("te", lambda t, si=si, pB=pB: t.matmul(
                pB[:], sel[:, si * 128:(si + 1) * 128], bcf[:, 0:Tc],
                start=True, stop=True),
                reads=["bcf", "sel"], writes=["ps2"])
            ctx.op("act", lambda s, si=si, pB=pB: s.copy(
                B_st[:, si * Tc:(si + 1) * Tc], pB[:]),
                reads=["ps2"], writes=["Bst"])
            ctx.op("te", lambda t, si=si, pC=pC: t.matmul(
                pC[:], sel[:, si * 128:(si + 1) * 128], bcf[:, Tc:2 * Tc],
                start=True, stop=True),
                reads=["bcf", "sel"], writes=["ps3"])
            ctx.op("act", lambda s, si=si, pC=pC: s.copy(
                C_st[:, si * Tc:(si + 1) * Tc], pC[:]),
                reads=["ps3"], writes=["Cst"])
        for g in range(ND):
            dug = du[:, g * Tc:(g + 1) * Tc]
            dug_b = bass.AP(dug.tensor, dug.offset,
                            [[ND * Tc, 128], [0, DST], [1, Tc]])
            ctx.op("dve", lambda v, g=g, dug_b=dug_b: v.tensor_mul(
                b_stk[:].rearrange("p (s t) -> p s t", t=Tc), dug_b,
                B_st[:].rearrange("p (s t) -> p s t", t=Tc)),
                reads=[f"du{g}", "Bst"], writes=["bstk"])
            for si in range(DST):
                ib = si % 2
                av = a_t[ib]
                st_res = f"st{g}_{si}"
                ctx.op("act", lambda s, g=g, si=si, av=av: s.activation(
                    av[:], delta[:, g * Tc:(g + 1) * Tc], AF.Exp,
                    scale=ascan[:, g * DST + si: g * DST + si + 1]),
                    reads=[f"dl{g}"], sreads=["ascan"], writes=[f"a{ib}"])
                init_arg = 0.0 if c == 0 else state[:, g * DST + si: g * DST + si + 1]
                ctx.op("dve", lambda v, g=g, si=si, av=av, init_arg=init_arg:
                       v.tensor_tensor_scan(
                           h_st[:, si * Tc:(si + 1) * Tc], av[:],
                           b_stk[:, si * Tc:(si + 1) * Tc], init_arg,
                           ALU.mult, ALU.add),
                       reads=[f"a{ib}", "bstk"], sreads=([st_res] if c > 0 else []),
                       writes=["hst"], late=["hst"])
            ctx.op("dve", lambda v, g=g: v.tensor_copy(
                state[:, g * DST:(g + 1) * DST],
                h_st[:].rearrange("p (s t) -> p s t", t=Tc)[:, :, Tc - 1:Tc]
                .rearrange("p s t -> p (s t)")),
                reads=["hst"], writes=[f"st{g}_{si2}" for si2 in range(DST)],
                late=[f"st{g}_{si2}" for si2 in range(DST)])
            ctx.op("dve", lambda v, g=g: v.tensor_mul(hc_st[:], h_st[:], C_st[:]),
                   reads=["hst", "Cst"], writes=["hst"])
            w = DST * Tc
            while w > Tc:
                hw = w // 2
                ctx.op("dve", lambda v, hw=hw, w=w: v.tensor_add(
                    hc_st[:, 0:hw], hc_st[:, 0:hw], hc_st[:, hw:w]),
                    reads=["hst"], writes=["hst"])
                w = hw
            wb = winb[g % 2]
            wb_r = f"winb{g % 2}"
            ctx.dma("qs", wb[:], win_bf[ND + g], reads=[f"winbf{ND + g}"],
                    writes=[wb_r])
            pz = PS[4 + g % 2]
            pz_r = f"ps{4 + g % 2}"
            for j in range(NE):
                ctx.op("te", lambda t, g=g, j=j, pz=pz, wb=wb: t.matmul(
                    pz[:], wb[:, j * 128:(j + 1) * 128], lnx[j][:],
                    start=(j == 0), stop=(j == NE - 1)),
                    reads=[f"lnx{j}", wb_r], writes=[pz_r])
            zb = zrec[g % 2]
            ctx.op("act", lambda s, g=g, pz=pz, zb=zb: s.activation(
                zb[:], pz[:], AF.Silu),
                reads=[pz_r], writes=["zr0"])
            ctx.op("dve", lambda v, g=g: v.scalar_tensor_tensor(
                hc_st[:, 0:Tc], xc[:, g * Tc:(g + 1) * Tc],
                dP[:, g:g + 1], hc_st[:, 0:Tc], ALU.mult, ALU.add),
                reads=["xc", "hst"], sreads=["dP"], writes=["hst"])
            ctx.op("dve", lambda v, g=g, zb=zb: v.tensor_mul(
                yz[:, g * Tc:(g + 1) * Tc], hc_st[:, 0:Tc], zb[:]),
                reads=["hst", "zr0", f"du{g}"], writes=["yz", f"du{g}"])

        for fo in range(NE):
            wb = woutb[fo % 2]
            wb_r = f"woutb{fo % 2}"
            ctx.dma("qs", wb[:], wout_bf[fo], reads=[f"woutbf{fo}"], writes=[wb_r])
            pb = PS[fo % 2]
            pr = f"ps{fo % 2}"
            for g in range(ND):
                ctx.op("te", lambda t, fo=fo, g=g, pb=pb, wb=wb: t.matmul(
                    pb[:], wb[:, g * 128:(g + 1) * 128], yz[:, g * Tc:(g + 1) * Tc],
                    start=(g == 0), stop=(g == ND - 1)),
                    reads=["yz"] + [wb_r], writes=[pr])
            ctx.op("act", lambda s, fo=fo, pb=pb: s.copy(mo[fo][:], pb[:]),
                   reads=[pr], writes=[f"mo{fo}", f"lnx{fo}"])
            ctx.op("dve", lambda v, fo=fo: v.tensor_add(h1[fo][:], xt[fo][:], mo[fo][:]),
                   reads=[f"mo{fo}", f"x{fo}"], writes=[f"h1{fo}"])
        emit_ln(p, f"G{c}_", h1, "h1", gout, "gout", scr, gamma=gmb, beta=btb,
                stats_from=None, xbf_is_out=True)
        emit_ln_residual(p, f"R{c}_", gout, "gout", xt, "x", scr)
        for j in range(NE):
            ctx.dma("qs", yT[j * 128:(j + 1) * 128, t0:t0 + Tc], xt[j][:],
                    reads=[f"x{j}"], writes=[f"y{j}"])
        if c == NCH - 1:
            ctx.barrier()
    return p.close() if standalone else None



def build_stage_l1ffn(p=None, io=None):
    standalone = p is None
    if standalone:
        p = Prog("l1ffn").__enter__()
    nc, ctx = p.nc, p.ctx
    xT = io["xT"] if io else p.din("xT", [E, S], F32)
    WrT = p.din("WrT", [E, E], F32)
    WkT = p.din("WkT", [E, E], F32)
    WvT = p.din("WvT", [E, E], F32)
    mu_rP = p.din("mu_rP", [128, NE], F32)
    mu_kP = p.din("mu_kP", [128, NE], F32)
    yT = io["yT"] if io else p.dout("yT", [E, S], F32)

    setup_common(p)
    PS = p.psum
    wr = p.sb("wr", [128, NE * E], BF16)
    wk = p.sb("wk", [128, NE * E], BF16)
    wv = p.sb("wv", [128, NE * E], BF16)
    mur = p.sb("mur", [128, NE], F32)
    muk = p.sb("muk", [128, NE], F32)
    for j in range(NE):
        ctx.dma("pool", wr[:, j * E:(j + 1) * E], WrT[j * 128:(j + 1) * 128, :],
                writes=[f"wr{j}"])
        ctx.dma("pool", wk[:, j * E:(j + 1) * E], WkT[j * 128:(j + 1) * 128, :],
                writes=[f"wk{j}"])
        ctx.dma("pool", wv[:, j * E:(j + 1) * E], WvT[j * 128:(j + 1) * 128, :],
                writes=[f"wv{j}"])
    ctx.dma("qs", mur[:], mu_rP, writes=["mur"])
    ctx.dma("qs", muk[:], mu_kP, writes=["muk"])

    T1 = Tc + 1
    xt = [p.sb(f"xt{j}", [128, Tc], F32) for j in range(NE)]
    lnx = [p.sb(f"lnxx{j}", [128, T1], BF16) for j in range(NE)]
    scr = make_ln_scratch(p)
    scr["psA"], scr["psB"], scr["psC"] = PS[4], PS[5], PS[6]
    scr["psA_r"], scr["psB_r"], scr["psC_r"] = "ps4", "ps5", "ps6"
    dsh = p.sb("dsh", [128, Tc], BF16)
    xr = [p.sb(f"xr{j}", [128, Tc], BF16) for j in range(NE)]
    xk = [p.sb(f"xk{j}", [128, Tc], BF16) for j in range(NE)]
    rb = [p.sb(f"rbs{j}", [128, Tc], BF16) for j in range(NE)]
    k2 = [p.sb(f"k2{j}", [128, Tc], BF16) for j in range(NE)]
    krelu = p.sb("krelu", [128, Tc], BF16)
    obf = [p.sb(f"obf{j}", [128, Tc], BF16) for j in range(NE)]

    lnx_out = [lnx[j][:, 1:T1] for j in range(NE)]
    for c in range(NCH):
        t0 = c * Tc
        for j in range(NE):
            if c == 0:
                ctx.op("pool", lambda g, j=j: g.memset(lnx[j][:, 0:1], 0.0),
                       writes=[f"lnp{j}"])
            else:
                ctx.op("dve", lambda v, j=j: v.tensor_copy(lnx[j][:, 0:1],
                                                           lnx[j][:, Tc:T1]),
                       reads=[f"lnx{j}"], writes=[f"lnp{j}"])
            ctx.dma("pool", xt[j][:], xT[j * 128:(j + 1) * 128, t0:t0 + Tc],
                    writes=[f"x{j}"])
        emit_ln(p, f"L{c}_", xt, "x", lnx_out, "lnx", scr)

        for j in range(NE):
            ctx.op("dve", lambda v, j=j: v.tensor_sub(dsh[:], lnx[j][:, 1:T1],
                                                      lnx[j][:, 0:Tc]),
                   reads=[f"lnx{j}", f"lnp{j}"], writes=["dsh"])
            ctx.op("dve", lambda v, j=j: v.scalar_tensor_tensor(
                xr[j][:], dsh[:], mur[:, j:j + 1], lnx[j][:, 0:Tc], ALU.mult, ALU.add),
                reads=["dsh", f"lnx{j}"], sreads=["mur"], writes=[f"xr{j}"])
            ctx.op("dve", lambda v, j=j: v.scalar_tensor_tensor(
                xk[j][:], dsh[:], muk[:, j:j + 1], lnx[j][:, 0:Tc], ALU.mult, ALU.add),
                reads=["dsh", f"lnx{j}"], sreads=["muk"], writes=[f"xk{j}"])

        for fo in range(NE):
            pb = PS[fo % 2]
            pr = f"ps{fo % 2}"
            for j in range(NE):
                ctx.op("te", lambda t, fo=fo, j=j, pb=pb: t.matmul(
                    pb[:, 0:Tc], wr[:, j * E + fo * 128: j * E + (fo + 1) * 128],
                    xr[j][:], start=(j == 0), stop=(j == NE - 1)),
                    reads=[f"xr{j}", f"wr{j}"], writes=[pr])
            ctx.op("act", lambda s, fo=fo, pb=pb: s.activation(
                rb[fo][:], pb[:, 0:Tc], AF.Sigmoid),
                reads=[pr], writes=[f"rb{fo}"])
        for fo in range(NE):
            pb = PS[2 + fo % 2]
            pr = f"ps{2 + fo % 2}"
            for j in range(NE):
                ctx.op("te", lambda t, fo=fo, j=j, pb=pb: t.matmul(
                    pb[:, 0:Tc], wk[:, j * E + fo * 128: j * E + (fo + 1) * 128],
                    xk[j][:], start=(j == 0), stop=(j == NE - 1)),
                    reads=[f"xk{j}", f"wk{j}"], writes=[pr])
            ctx.op("act", lambda s, fo=fo, pb=pb: s.activation(
                krelu[:], pb[:, 0:Tc], AF.Relu),
                reads=[pr], writes=["krelu"])
            ctx.op("act", lambda s, fo=fo: s.activation(
                k2[fo][:], krelu[:], AF.Square),
                reads=["krelu"], writes=[f"k2{fo}"])
        for fo in range(NE):
            pb = PS[fo % 2]
            pr = f"ps{fo % 2}"
            for j in range(NE):
                ctx.op("te", lambda t, fo=fo, j=j, pb=pb: t.matmul(
                    pb[:, 0:Tc], wv[:, j * E + fo * 128: j * E + (fo + 1) * 128],
                    k2[j][:], start=(j == 0), stop=(j == NE - 1)),
                    reads=[f"k2{j}", f"wv{j}"], writes=[pr])
            ctx.op("dve", lambda v, fo=fo, pb=pb: v.tensor_mul(
                obf[fo][:], pb[:, 0:Tc], rb[fo][:]),
                reads=[pr, f"rb{fo}"], writes=[f"obf{fo}"])

        emit_ln_residual(p, f"R{c}_", obf, "obf", xt, "x", scr)
        for j in range(NE):
            ctx.dma("qs", yT[j * 128:(j + 1) * 128, t0:t0 + Tc], xt[j][:],
                    reads=[f"x{j}"], writes=[f"y{j}"])
        ctx.barrier()
    return p.close() if standalone else None


def build_fused():
    p = Prog("fused").__enter__()
    nc = p.nc
    xT = p.din("xT", [E, S], F32)
    x1 = nc.dram_tensor("x1", [E, S], F32, kind="Internal").ap()
    x2 = nc.dram_tensor("x2", [E, S], F32, kind="Internal").ap()
    x3 = nc.dram_tensor("x3", [E, S], F32, kind="Internal").ap()
    yT = p.dout("yT", [E, S], F32)
    setup_common(p)
    hops = [xT, x1, x2, x3, yT]
    fns = [build_stage_l0attn, build_stage_l0ffn, build_stage_l1mamba,
           build_stage_l1ffn]
    for k, fn in enumerate(fns):
        p.push_scope()
        fn(p=p, io={"xT": hops[k], "yT": hops[k + 1]})
        p.pop_scope()
    return p.close()



def _prep_l0attn(inputs):
    w_in = inputs["L0_attn_w_in"]
    b_in = inputs["L0_attn_b_in"]
    w_out = inputs["L0_attn_w_out"]
    b_out = inputs["L0_attn_b_out"]
    return {
        "w_qkT": np.ascontiguousarray(w_in[:2048].T),
        "w_vT": np.ascontiguousarray(w_in[2048:].T),
        "b_inP": np.ascontiguousarray(b_in.reshape(24, 128).T),
        "w_oT": np.ascontiguousarray(w_out.T),
        "b_oP": np.ascontiguousarray(b_out.reshape(8, 128).T),
    }


def _prep_l0ffn(inputs):
    return {
        "wgT": np.ascontiguousarray(inputs["L0_ffn_w_gate"].T),
        "wuT": np.ascontiguousarray(inputs["L0_ffn_w_up"].T),
        "wdT": np.ascontiguousarray(inputs["L0_ffn_w_down"].T),
    }


def _perpart(v, n):
    return np.ascontiguousarray(np.asarray(v).reshape(n, 128).T)


def _prep_l1mamba(inputs):
    A_log = np.asarray(inputs["L1_attn_A_log"])
    conv_w = np.asarray(inputs["L1_attn_conv_w"])
    return {
        "W_in": np.ascontiguousarray(inputs["L1_attn_W_in"]),
        "conv_wP": np.ascontiguousarray(conv_w.reshape(16, 128, 4).transpose(1, 0, 2)
                                        .reshape(128, 64)),
        "conv_bP": _perpart(inputs["L1_attn_conv_b"], 16),
        "W_x": np.ascontiguousarray(inputs["L1_attn_W_x"]),
        "W_dt": np.ascontiguousarray(inputs["L1_attn_W_dt"]),
        "b_dtP": _perpart(inputs["L1_attn_b_dt"], 16),
        "A_logP": np.ascontiguousarray(A_log.reshape(16, 128, 16).transpose(1, 0, 2)
                                       .reshape(128, 256)),
        "D_P": _perpart(inputs["L1_attn_D"], 16),
        "W_out": np.ascontiguousarray(inputs["L1_attn_W_out"]),
        "gammaP": _perpart(inputs["L1_attn_gamma"], 8),
        "betaP": _perpart(inputs["L1_attn_beta"], 8),
    }


def _prep_l1ffn(inputs):
    return {
        "WrT": np.ascontiguousarray(inputs["L1_ffn_W_r"].T),
        "WkT": np.ascontiguousarray(inputs["L1_ffn_W_k"].T),
        "WvT": np.ascontiguousarray(inputs["L1_ffn_W_v"].T),
        "mu_rP": _perpart(inputs["L1_ffn_mu_r"], 8),
        "mu_kP": _perpart(inputs["L1_ffn_mu_k"], 8),
    }


STAGES = [
    (build_stage_l0attn, _prep_l0attn),
    (build_stage_l0ffn, _prep_l0ffn),
    (build_stage_l1mamba, _prep_l1mamba),
    (build_stage_l1ffn, _prep_l1ffn),
]


FUSED = True


def kernel(**inputs):
    x = inputs["L0_attn_x"]
    xT = [np.ascontiguousarray(x[b].T) for b in range(NCORE)]

    if FUSED:
        common = {}
        for _, prep in STAGES:
            common.update(prep(inputs))
        nc = build_fused()
        maps = [dict(xT=xT[b], **common) for b in range(NCORE)]
        res = run_bass_kernel_spmd(nc, maps, core_ids=list(range(NCORE)))
        xT = [res.results[b]["yT"] for b in range(NCORE)]
    else:
        for build, prep in STAGES:
            common = prep(inputs)
            nc = build()
            maps = [dict(xT=xT[b], **common) for b in range(NCORE)]
            res = run_bass_kernel_spmd(nc, maps, core_ids=list(range(NCORE)))
            xT = [np.ascontiguousarray(res.results[b]["yT"]) for b in range(NCORE)]

    out = np.stack([xT[b].T for b in range(NCORE)], axis=0).astype(np.float32)
    return out
